# version 11
# baseline (speedup 1.0000x reference)
import numpy as np
import concourse.bass as bass
import concourse.mybir as mybir
from concourse.bass_utils import run_bass_kernel_spmd

F32 = mybir.dt.float32
BF16 = mybir.dt.bfloat16
AF = mybir.ActivationFunctionType
OP = mybir.AluOpType
AX = mybir.AxisListType

R = 4
L = 2048
DM = 1024
DI = 2048
S = 16
NI = 2816
T = 128
G = 512
NG = L // G
NCH = G // T
MASKV = 1.0e30
DBG_OFF = {}


class P:
    ENGS = ("tensor", "vector", "scalar", "gpsimd", "sync")

    def __init__(self, nc):
        self.nc = nc
        self.streams = {e: [] for e in self.ENGS}
        self.sems = {}
        self.cnt = {}
        self.last = {e: None for e in self.ENGS}

    def sem(self, name):
        if name not in self.sems:
            self.sems[name] = self.nc.alloc_semaphore(name)
            self.cnt[name] = 0
        return self.sems[name]

    def op(self, eng, fn, inc=None, dinc=None):
        if eng != "tensor" and self.last[eng] is not None:
            ls, lv = self.last[eng]
            h0 = self.sem(ls)
            self.streams[eng].append(lambda e, h0=h0, v=lv: e.wait_ge(h0, v))
        if dinc is not None:
            h = self.sem(dinc)
            self.cnt[dinc] += 16
            self.streams[eng].append(lambda e, fn=fn, h=h: fn(e).then_inc(h, 16))
            return self.cnt[dinc]
        sname = inc if inc is not None else "E" + eng
        h = self.sem(sname)
        self.cnt[sname] += 1
        self.streams[eng].append(lambda e, fn=fn, h=h: fn(e).then_inc(h, 1))
        self.last[eng] = (sname, self.cnt[sname])
        return self.cnt[sname]

    def wait(self, eng, sname, val=None):
        if val is None:
            val = self.cnt.get(sname, 0)
        if val <= 0:
            return
        h = self.sem(sname)
        self.streams[eng].append(lambda e, h=h, v=val: e.wait_ge(h, v))

    def c(self, sname):
        return self.cnt.get(sname, 0)


def build_program():
    nc = bass.Bass("TRN2", target_bir_lowering=False)
    p = P(nc)

    d_x = nc.dram_tensor("x", [R, L, DM], F32, kind="ExternalInput").ap()
    d_xT = nc.dram_tensor("xT", [R, DM, L], F32, kind="ExternalInput").ap()
    d_winT = nc.dram_tensor("winT", [DM, 2 * DI], F32, kind="ExternalInput").ap()
    d_wxT = nc.dram_tensor("wxT", [DI, 72], F32, kind="ExternalInput").ap()
    d_woutT = nc.dram_tensor("woutT", [DI, DM], F32, kind="ExternalInput").ap()
    d_cw = nc.dram_tensor("cw", [128, 16, 4], F32, kind="ExternalInput").ap()
    d_cb = nc.dram_tensor("cb", [128, 16], F32, kind="ExternalInput").ap()
    d_Dm = nc.dram_tensor("Dm", [128, 16], F32, kind="ExternalInput").ap()
    d_alog = nc.dram_tensor("alog", [DI, S], F32, kind="ExternalInput").ap()
    d_wg = nc.dram_tensor("wg", [NI, DM], F32, kind="ExternalInput").ap()
    d_wu = nc.dram_tensor("wu", [NI, DM], F32, kind="ExternalInput").ap()
    d_wd = nc.dram_tensor("wd", [DM, NI], F32, kind="ExternalInput").ap()
    d_out = nc.dram_tensor("out", [R, L, DM], F32, kind="ExternalOutput").ap()
    s_winTb = nc.dram_tensor("s_winTb", [DM, 2 * DI], BF16).ap()
    s_xc = nc.dram_tensor("s_xc", [R, DI, L], BF16).ap()
    s_x1 = nc.dram_tensor("s_x1", [R, L, DM], F32).ap()
    s_xq = nc.dram_tensor("s_xq", [R, L, DM], BF16).ap()
    s_wqg = nc.dram_tensor("s_wqg", [NI, DM], BF16).ap()
    s_wqu = nc.dram_tensor("s_wqu", [NI, DM], BF16).ap()
    s_wqd = nc.dram_tensor("s_wqd", [DM, NI], BF16).ap()
    s_hq = nc.dram_tensor("s_hq", [2, G, NI], BF16).ap()

    import ml_dtypes as _mld
    mask_np = np.where(np.arange(T)[:, None] > np.arange(T)[None, :],
                       np.float32(MASKV), np.float32(0.0)).astype(_mld.bfloat16)
    d_mask = nc.inline_tensor(mask_np, "maskut").ap()
    d_ones = nc.inline_tensor(np.ones((1, 128), np.float32), "onesr").ap()
    d_neg1 = nc.inline_tensor(-np.ones((1, 128), np.float32), "negr").ap()
    d_id = nc.inline_tensor(np.eye(128, dtype=np.float32), "idf").ap()

    ARENA_B = 209920
    arena = nc.alloc_sbuf_tensor("arena", [128, ARENA_B // 2], BF16)
    arena_ap = arena.ap()
    cur = [0]

    def reg(nbytes, shape, dtype, at=None):
        import traceback as _tb
        nbytes = (nbytes + 31) // 32 * 32
        off = cur[0] if at is None else at
        DBG_OFF.setdefault(_tb.extract_stack()[-2].lineno, []).append(off)
        if at is None:
            cur[0] += nbytes
            assert cur[0] <= ARENA_B, f"arena overflow {cur[0]}"
        n = int(np.prod(shape[1:]))
        esz = 4 if dtype == F32 else 2
        assert n * esz <= nbytes
        ap = arena_ap[:shape[0], off // 2:(off + ((n * esz + 3) // 4) * 4) // 2]
        if dtype == F32:
            ap = ap.bitcast(F32)
        ap = ap[:, 0:n]
        if len(shape) > 2:
            dims = " ".join(f"d{i}" for i in range(1, len(shape)))
            ap = ap.rearrange(f"p ({dims}) -> p {dims}",
                              **{f"d{i}": shape[i] for i in range(1, len(shape))})
        return ap

    q_xn = reg(4096, [128, 1024], F32)
    q_c1 = reg(2048, [128, 1024], BF16)
    q_c2 = reg(2048, [128, 1024], BF16)
    q_c3 = reg(2048, [128, 1024], BF16)
    q_sgn = reg(2048, [128, 1024], BF16)
    bm_f = reg(256, [128, 64], F32)
    sc_f = reg(256, [128, 64], F32)
    scb = reg(128, [128, 64], BF16)
    inv_f = reg(256, [128, 64], F32)
    wraw = reg(1024 * 4, [128, 1024], F32)
    wq_t = reg(1024 * 2, [128, 1024], BF16)

    ab_base = cur[0]

    NRING = 6
    wring = [reg(8 * 128 * 2, [128, 8, 128], BF16) for _ in range(NRING)]
    woutT = reg(16 * 1024 * 2, [128, 16, 1024], BF16)
    xT2 = [reg(8 * 512 * 2, [128, 8, 512], BF16) for _ in range(2)]
    xi_b = reg(16 * 516 * 2, [128, 16, 516], BF16)
    xcs_b = reg(16 * 512 * 2, [128, 16, 512], BF16)
    z_b = reg(16 * 512 * 2, [128, 16, 512], BF16)
    xcT4 = [reg(DI * 2, [128, DI], BF16) for _ in range(4)]
    x2_2 = [reg(DI * 2, [128, DI], BF16) for _ in range(2)]
    dall = reg(1024 * 4, [128, 1024], F32)
    eall = reg(1024 * 2, [128, 1024], BF16)
    p_sb = reg(1024 * 2, [128, 1024], BF16)
    dterm2 = [reg(2048 * 2, [128, 2048], BF16) for _ in range(2)]
    yT_sb = reg(2048 * 2, [128, 2048], BF16)
    x_c2 = [reg(4096, [128, 1024], F32) for _ in range(2)]
    bcrow_t = reg(12288, [128, 6144], BF16)
    bcrow = bcrow_t[0:1, :]
    wxT = reg(16 * 72 * 2, [128, 16, 72], BF16)
    cw = reg(256, [128, 16, 4], F32)
    cb = reg(64, [128, 16], F32)
    Dm = reg(64, [128, 16], F32)
    A_bc = reg(64, [128, 16], F32)
    maskut = reg(256, [128, 128], BF16)
    onesr_t = reg(512, [128, 128], F32)
    onesr = onesr_t[0:1, :]
    neg1r_t = reg(512, [128, 128], F32)
    neg1r = neg1r_t[0:1, :]
    idf = reg(512, [128, 128], F32)
    arow_t = reg(64, [128, 16], F32)
    arow16 = arow_t[0:1, :]
    tiny = reg(128, [128, 32], F32)
    H_sb = reg(DI * 2, [128, DI], BF16)[32:32 + S, :]
    ssmT_t = reg(512, [128, 128], F32)
    ssmT = ssmT_t[0:72, :]
    ssm_f = reg(288, [128, 72], F32)
    es_f = reg(512, [128, 128], F32)[32:48, :]
    efac = reg(64, [128, 16], F32)
    ceT2 = [reg(256, [128, 128], BF16)[32:48, :] for _ in range(2)]
    bex2 = [reg(64, [128, 16], BF16) for _ in range(2)]
    dcol2 = reg(32, [128, 8], F32)
    dl_f = reg(512, [128, 128], F32)
    m_2 = [reg(256, [128, 128], BF16) for _ in range(2)]
    crow_t = reg(512, [128, 128], F32)
    crow = crow_t[0:1, :]
    dtprow_t = reg(512, [128, 128], F32)
    dtprow = dtprow_t[0:1, :]
    dtraw_t = reg(512, [128, 128], F32)
    dtraw = dtraw_t[0:1, :]
    dtpcol = reg(32, [128, 8], F32)
    stat = reg(64, [128, 16], F32)
    a_end = cur[0]
    assert a_end <= ARENA_B, f"A {a_end}"

    cur[0] = ab_base
    wqgT = reg(8 * 2816 * 2, [128, 8, 2816], BF16)
    wquT = reg(8 * 2816 * 2, [128, 8, 2816], BF16)
    wqdT = reg(22 * 1024 * 2, [128, 22, 1024], BF16)
    xqTh = [reg(8 * 256 * 2, [128, 8, 256], BF16) for _ in range(2)]
    hqT_b = reg(22 * 512 * 2, [128, 22, 512], BF16)
    sg_sb = reg(4096, [128, 1024], F32)
    hq_sub = reg(2048, [128, 1024], BF16)
    x1l = reg(4096, [128, 1024], F32)
    x1q = reg(4096, [128, 1024], F32)
    out_sb = reg(4096, [128, 1024], F32)
    statB = reg(64, [128, 16], F32)
    b_end = cur[0]
    assert b_end <= ARENA_B, f"B {b_end}"
    xq_out = arena_ap[:, q_xn.stride() and 0:0]

    P1 = nc.alloc_psum_tensor("Ps1", [128, 1024], F32).ap()
    P2 = nc.alloc_psum_tensor("Ps2", [128, 1024], F32).ap()
    P3 = nc.alloc_psum_tensor("Ps3", [128, 1024], F32).ap()
    P4 = nc.alloc_psum_tensor("Ps4", [128, 512], F32).ap()
    P5 = nc.alloc_psum_tensor("Ps5", [128, 512], F32).ap()
    v_dl = P4[:, 0:128]
    v_ssmT = P4[0:72, 128:256]
    v_pe1 = P4[32:48, 256:384]
    v_pe2 = P4[:, 384:385]
    v_tr = P5[:, 0:128]

    def dma(qeng, dst, src, dname):
        return p.op(qeng, lambda e, d=dst, s=src: e.dma_start(d, s), dinc=dname)

    def tdma(qeng, dst, src, dname):
        return p.op(qeng, lambda e, d=dst, s=src:
                    e.dma_start(d, s, transpose=True), dinc=dname)

    dma("gpsimd", s_winTb, d_winT, "Dwtb")
    dma("gpsimd", wxT, d_wxT.rearrange("(a p) b -> p a b", p=128), "Dw")
    dma("gpsimd", woutT, d_woutT.rearrange("(a p) b -> p a b", p=128), "Dw")
    dma("sync", cw, d_cw, "Dc")
    dma("sync", cb, d_cb, "Dc")
    dma("sync", Dm, d_Dm, "Dc")
    dma("sync", tiny[0:1, 0:16], d_alog[0:1, :], "Dc")
    dma("sync", maskut, d_mask, "Dc")
    dma("sync", onesr, d_ones, "Dc")
    dma("sync", neg1r, d_neg1, "Dc")
    dma("sync", idf, d_id, "Dc")
    dma("gpsimd", xT2[0][:],
        d_xT[0, :, 0:G].rearrange("(a p) t -> p a t", p=128), "DxT0")

    p.wait("scalar", "Dc")
    p.op("scalar", lambda e: e.activation(tiny[0:1, 16:32], tiny[0:1, 0:16], AF.Exp),
         inc="Sc")
    p.wait("vector", "Sc")
    p.op("vector", lambda e: e.tensor_scalar(arow16[:, 0:16], tiny[0:1, 16:32],
                                             -1.0, None, OP.mult), inc="Vc")
    p.wait("tensor", "Vc")
    p.wait("tensor", "Dc")
    p.op("tensor", lambda e: e.matmul(P4[:, 0:16], onesr, arow16[:, 0:16],
                                      start=True, stop=True), inc="Tc")
    p.wait("vector", "Tc")
    p.op("vector", lambda e: e.tensor_copy(A_bc, P4[:, 0:16]), inc="Vc")

    def emit_quant(src_f32, dst_bf16, width):
        nb = width // 16
        p.op("vector", lambda e: e.tensor_reduce(
            bm_f[:, 0:nb], src_f32.rearrange("p (b k) -> p b k", k=16),
            AX.X, OP.max, True), inc="Vq")
        p.op("vector", lambda e: e.tensor_scalar(
            sc_f[:, 0:nb], bm_f[:, 0:nb], 1.0 / 6.0, 1e-12, OP.mult, OP.max),
             inc="Vq")
        p.wait("vector", "Vq")
        p.op("vector", lambda e: e.reciprocal(inv_f[:, 0:nb], sc_f[:, 0:nb]),
             inc="Vq")
        p.wait("vector", "Vq")
        p.op("vector", lambda e: e.tensor_copy(scb[:, 0:nb], sc_f[:, 0:nb]))
        w2 = (width // 2) // 16 * 16
        halves = [(0, w2), (w2, width - w2)] if width > 512 else [(0, width)]
        v = p.op
        src_done = 0
        sq_at = {}
        for hi, (o, w) in enumerate(halves):
            b0, b1 = o // 16, (o + w) // 16
            inv_bc = inv_f[:, b0:b1].rearrange("p b -> p b ()").broadcast_to(
                [128, b1 - b0, 16])
            src_done = p.op("vector", lambda e, o=o, w=w, inv_bc=inv_bc:
                            e.tensor_tensor(
                q_xn[:, o:o + w].rearrange("p (b k) -> p b k", k=16),
                src_f32[:, o:o + w].rearrange("p (b k) -> p b k", k=16),
                inv_bc, OP.mult), inc="Vq")
            p.wait("scalar", "Vq")
            p.op("scalar", lambda e, o=o, w=w: e.activation(
                q_sgn[:, o:o + w], q_xn[:, o:o + w], AF.Sign), inc="Sq")
            sq_at[hi] = p.op("scalar", lambda e, o=o, w=w: e.activation(
                q_xn[:, o:o + w], q_xn[:, o:o + w], AF.Abs), inc="Sq")

        def hop(fn, thr=None, comb=None):
            for hi, (o, w) in enumerate(halves):
                fn(hi, o, w)

        for hi, (o, w) in enumerate(halves):
            p.wait("vector", "Sq", sq_at[hi])
            v("vector", lambda e, o=o, w=w: e.tensor_scalar(
                q_c1[:, o:o + w], q_xn[:, o:o + w], 0.25, None, OP.is_gt))
        for hi, (o, w) in enumerate(halves):
            v("vector", lambda e, o=o, w=w: e.tensor_scalar(
                q_c2[:, o:o + w], q_xn[:, o:o + w], 0.75, None, OP.is_gt))
        for hi, (o, w) in enumerate(halves):
            v("vector", lambda e, o=o, w=w: e.tensor_tensor(
                q_c1[:, o:o + w], q_c1[:, o:o + w], q_c2[:, o:o + w], OP.add))
        for hi, (o, w) in enumerate(halves):
            v("vector", lambda e, o=o, w=w: e.tensor_scalar(
                q_c2[:, o:o + w], q_xn[:, o:o + w], 1.25, None, OP.is_gt))
        for hi, (o, w) in enumerate(halves):
            v("vector", lambda e, o=o, w=w: e.tensor_scalar(
                q_c3[:, o:o + w], q_xn[:, o:o + w], 1.75, None, OP.is_gt))
        for hi, (o, w) in enumerate(halves):
            v("vector", lambda e, o=o, w=w: e.tensor_tensor(
                q_c2[:, o:o + w], q_c2[:, o:o + w], q_c3[:, o:o + w], OP.add))
        for hi, (o, w) in enumerate(halves):
            v("vector", lambda e, o=o, w=w: e.tensor_tensor(
                q_c1[:, o:o + w], q_c1[:, o:o + w], q_c2[:, o:o + w], OP.add))
        for hi, (o, w) in enumerate(halves):
            v("vector", lambda e, o=o, w=w: e.tensor_scalar(
                q_c2[:, o:o + w], q_xn[:, o:o + w], 2.5, None, OP.is_gt))
        for hi, (o, w) in enumerate(halves):
            v("vector", lambda e, o=o, w=w: e.tensor_scalar(
                q_c3[:, o:o + w], q_xn[:, o:o + w], 3.5, None, OP.is_gt))
        for hi, (o, w) in enumerate(halves):
            v("vector", lambda e, o=o, w=w: e.tensor_tensor(
                q_c2[:, o:o + w], q_c2[:, o:o + w], q_c3[:, o:o + w], OP.add))
        for hi, (o, w) in enumerate(halves):
            v("vector", lambda e, o=o, w=w: e.tensor_scalar(
                q_c3[:, o:o + w], q_xn[:, o:o + w], 5.0, None, OP.is_gt))
        for hi, (o, w) in enumerate(halves):
            v("vector", lambda e, o=o, w=w: e.scalar_tensor_tensor(
                q_c3[:, o:o + w], q_c3[:, o:o + w], 2.0, q_c2[:, o:o + w],
                OP.mult, OP.add))
        for hi, (o, w) in enumerate(halves):
            v("vector", lambda e, o=o, w=w: e.scalar_tensor_tensor(
                q_c1[:, o:o + w], q_c1[:, o:o + w], 0.5, q_c3[:, o:o + w],
                OP.mult, OP.add))
        for hi, (o, w) in enumerate(halves):
            v("vector", lambda e, o=o, w=w: e.tensor_tensor(
                q_c1[:, o:o + w], q_c1[:, o:o + w], q_sgn[:, o:o + w], OP.mult))
        for hi, (o, w) in enumerate(halves):
            b0, b1 = o // 16, (o + w) // 16
            scb_bc = scb[:, b0:b1].rearrange("p b -> p b ()").broadcast_to(
                [128, b1 - b0, 16])
            v("vector", lambda e, o=o, w=w, scb_bc=scb_bc: e.tensor_tensor(
                dst_bf16[:, o:o + w].rearrange("p (b k) -> p b k", k=16),
                q_c1[:, o:o + w].rearrange("p (b k) -> p b k", k=16),
                scb_bc, OP.mult), inc="Vq")
        return src_done

    wq_tiles = []
    for dsrc, ddst, ntile, width in ((d_wg, s_wqg, NI // 128, DM),
                                     (d_wu, s_wqu, NI // 128, DM),
                                     (d_wd, s_wqd, DM // 128, NI)):
        nh = (width + 1023) // 1024
        for t_ in range(ntile):
            for h_ in range(nh):
                w0 = h_ * 1024
                ww = min(1024, width - w0)
                wq_tiles.append((dsrc, ddst, t_, w0, ww))

    def wq_tile_dma(idx):
        dsrc, ddst, t_, w0, ww = wq_tiles[idx]
        p.wait("gpsimd", "Vq")
        dma("gpsimd", wraw[:, 0:ww],
            dsrc[t_ * 128:(t_ + 1) * 128, w0:w0 + ww], "Dwr")

    def wq_tile_quant(idx):
        dsrc, ddst, t_, w0, ww = wq_tiles[idx]
        p.wait("vector", "Dwr", 16 * (idx + 1))
        p.wait("vector", "Dwq")
        emit_quant(wraw[:, 0:ww], wq_t[:, 0:ww], ww)
        p.wait("sync", "Vq")
        dma("sync", ddst[t_ * 128:(t_ + 1) * 128, w0:w0 + ww],
            wq_t[:, 0:ww], "Dwq")

    NWQPRO = len(wq_tiles) - 48
    wq_tile_dma(0)
    for i_ in range(NWQPRO):
        wq_tile_quant(i_)
        if i_ + 1 < len(wq_tiles):
            wq_tile_dma(i_ + 1)

    hist = {"Vr": [], "Vx2": [], "DxcT": [], "Dxn": [], "Ty": [], "Dx1": [],
            "Vm": [], "Vce": [], "Vdt": [], "Vy": [],
            "nxT": {0: 16}, "nwg": {}}

    def ln_tail(st, rin, xout):
        v = p.op
        v("vector", lambda e: e.tensor_scalar(st[:, 2:3], st[:, 0:1],
                                              1.0 / 1024, None, OP.mult))
        v("vector", lambda e: e.tensor_scalar(st[:, 3:4], st[:, 1:2],
                                              1.0 / 1024, None, OP.mult))
        v("vector", lambda e: e.tensor_scalar(st[:, 4:5], st[:, 2:3],
                                              st[:, 2:3], None, OP.mult))
        v("vector", lambda e: e.tensor_tensor(st[:, 4:5], st[:, 3:4],
                                              st[:, 4:5], OP.subtract))
        v("vector", lambda e: e.tensor_scalar(st[:, 4:5], st[:, 4:5],
                                              1e-5, None, OP.add))
        v("vector", lambda e: e.memset(st[:, 5:6], 1.0))
        for _ in range(5):
            v("vector", lambda e: e.tensor_scalar(st[:, 6:7], st[:, 5:6],
                                                  st[:, 5:6], None, OP.mult))
            v("vector", lambda e: e.tensor_tensor(st[:, 6:7], st[:, 4:5],
                                                  st[:, 6:7], OP.mult))
            v("vector", lambda e: e.tensor_scalar(st[:, 6:7], st[:, 6:7],
                                                  -0.5, 1.5, OP.mult, OP.add))
            v("vector", lambda e: e.tensor_tensor(st[:, 5:6], st[:, 5:6],
                                                  st[:, 6:7], OP.mult))
        v("vector", lambda e: e.tensor_scalar(st[:, 7:8], st[:, 2:3],
                                              st[:, 5:6], -1.0, OP.mult, OP.mult),
          inc="Vln")
        p.wait("scalar", "Vln")
        p.op("scalar", lambda e: e.activation(xout, rin, AF.Identity,
                                              scale=st[:, 5:6], bias=st[:, 7:8]),
             inc="Sx1")

    def wring_load(gi, m):
        gm = gi * 32 + m
        if gm >= 8:
            p.wait("scalar", "Tmx", gm - 8 + 1)
        if gm == 0:
            p.wait("scalar", "Dwtb")
        hist["nwg"][gm] = dma(
            "scalar", wring[m % 8][:],
            s_winTb[:, m * 128:(m + 1) * 128].rearrange("(a p) c -> p a c", p=128),
            f"Dwg{m % 8}")

    plan = [(0, m) for m in range(32)]
    for g_ in range(16):
        for cc_ in range(4):
            a_ = [(g_ + 1, 4 * cc_ + j) for j in range(4)] if g_ + 1 < 16 else []
            plan += a_[:2]
            if cc_ == 1 and g_ >= 1:
                plan += [(g_, 16 + k) for k in range(16)]
            plan += a_[2:]
    plan_idx = {sec: i for i, sec in enumerate(plan)}
    idx_last = {}
    for i, (gi_, m_) in enumerate(plan):
        idx_last[gi_] = i

    xz_evac_last = [None, None]
    sec_state = {}
    evac_pend = []

    def wring_load(i):
        gi, m = plan[i]
        qe = "scalar" if (i % NRING) % 2 == 0 else "gpsimd"
        if i >= NRING:
            p.wait(qe, "Tmx", i - NRING + 1)
        if i <= 1:
            p.wait(qe, "Dwtb")
        hist["nwg"][i] = dma(
            qe, wring[i % NRING][:],
            s_winTb[:, m * 128:(m + 1) * 128].rearrange("(a p) c -> p a c", p=128),
            f"Dwg{i % NRING}")

    def emit_sec_mm(i):
        gi, m = plan[i]
        half = i % 2
        pdst = P1[:, half * 512:(half + 1) * 512]
        if xz_evac_last[half] is not None:
            p.wait("tensor", *xz_evac_last[half])
        p.wait("tensor", f"Dwg{i % NRING}", hist["nwg"][i])
        p.wait("tensor", f"DxT{gi % 2}", hist["nxT"][gi])
        for k in range(8):
            p.op("tensor", lambda e, k=k, i=i, pdst=pdst, buf=gi % 2:
                 e.matmul(pdst, wring[i % NRING][:, k, :], xT2[buf][:, k, :],
                          start=(k == 0), stop=(k == 7)),
                 inc=("Tmx" if k == 7 else None))
        sec_state[i] = (half, p.c("Tmx"))
        if i + 4 < len(plan):
            wring_load(i + 4)

    def emit_sec_evac(i):
        gi, m = plan[i]
        half, tmx = sec_state.pop(i)
        pdst = P1[:, half * 512:(half + 1) * 512]
        if m < 16:
            p.wait("vector", "Tmx", tmx)
            cn = p.op("vector", lambda e, m=m, pdst=pdst:
                      e.tensor_copy(xi_b[:, m, 4:516], pdst), inc="Vxz")
            xz_evac_last[half] = ("Vxz", cn)
        else:
            p.wait("scalar", "Tmx", tmx)
            if m == 16:
                p.wait("scalar", "Vyf", 4 * gi)
            cn = p.op("scalar", lambda e, m=m, pdst=pdst:
                      e.activation(z_b[:, m - 16, :], pdst, AF.Silu), inc="Szv")
            xz_evac_last[half] = ("Szv", cn)

    def drain_evacs():
        while evac_pend:
            emit_sec_evac(evac_pend.pop(0))

    def xi_pair(g, cc, which):
        if g + 1 >= 16:
            return
        for j in (0, 1):
            i = plan_idx[(g + 1, 4 * cc + 2 * which + j)]
            emit_sec_mm(i)
            evac_pend.append(i)

    def z_batch(g):
        base = plan_idx[(g, 16)]
        for k in range(16):
            emit_sec_mm(base + k)
            if k >= 1:
                emit_sec_evac(base + k - 1)
        emit_sec_evac(base + 15)

    def conv_tail(gidx):
        r, j = divmod(gidx, NG)
        p.wait("vector", "Tsm")
        p.wait("vector", "Dxc")
        for g in range(16):
            p.op("vector", lambda e, g=g: e.tensor_scalar(
                xcs_b[:, g, :], xi_b[:, g, 1:513], cw[:, g, 0:1], cb[:, g:g + 1],
                OP.mult, OP.add))
            for tap in range(1, 4):
                inc = "Vcv" if (tap == 3 and g == 15) else None
                p.op("vector", lambda e, g=g, tap=tap: e.scalar_tensor_tensor(
                    xcs_b[:, g, :], xi_b[:, g, 1 + tap:513 + tap],
                    cw[:, g, tap:tap + 1], xcs_b[:, g, :], OP.mult, OP.add), inc=inc)
        p.wait("scalar", "Vcv")
        p.op("scalar", lambda e: e.activation(
            xcs_b.rearrange("p a t -> p (a t)"),
            xcs_b.rearrange("p a t -> p (a t)"), AF.Silu), inc="Scv")
        p.wait("sync", "Scv")
        dma("sync", s_xc[r, :, j * G:(j + 1) * G].rearrange("(a p) t -> p a t", p=128),
            xcs_b, "Dxc")
        p.wait("sync", "Dxc")
        for cc in range(NCH):
            gc = gidx * NCH + cc
            if gc >= 4:
                p.wait("sync", "Vx2", hist["Vx2"][gc - 4])
            gtok = j * G + cc * T
            n = tdma("sync", xcT4[gc % 4][:], s_xc[r, :, gtok:gtok + T],
                     f"DxcT{gc % 4}")
            hist["DxcT"].append((f"DxcT{gc % 4}", n))

    def prep_aT(gc):
        gidx, cc = divmod(gc, NCH)
        t0 = cc * T
        p.wait("tensor", "Vdl")
        p.wait("tensor", "Scv")
        p.wait("tensor", "Ses")
        p.wait("tensor", "Vpe")
        for k in range(16):
            p.op("tensor", lambda e, k=k, t0=t0: e.matmul(
                v_ssmT, wxT[:, k, :], xcs_b[:, k, t0:t0 + T],
                start=(k == 0), stop=(k == 15)), inc=("Tsm" if k == 15 else None))

    def prep_b(gc):
        par = gc % 2
        p.wait("vector", "Tsm")
        p.op("vector", lambda e: e.tensor_copy(ssmT, v_ssmT), inc="Vsm")
        p.wait("gpsimd", "Vsm")
        p.wait("gpsimd", "Tcb")
        p.wait("gpsimd", "Vsp")
        p.wait("gpsimd", "Ssp")
        dma("gpsimd", bcrow.rearrange("p (a c) -> p a c", c=128),
            ssmT_t[0:48, :], "Dbc")
        dma("gpsimd", dtraw, ssmT_t[64:65, :], "Dbc")
        p.wait("tensor", "Vsm")
        p.op("tensor", lambda e: e.transpose(v_tr, ssmT_t, idf), inc="Tst")
        p.wait("vector", "Tst")
        p.op("vector", lambda e: e.tensor_copy(ssm_f[:, 0:72], v_tr[:, 0:72]),
             inc="Vsm")
        p.wait("scalar", "Vsm")
        p.wait("scalar", "Dbc")
        p.op("scalar", lambda e: e.activation(dtprow, dtraw, AF.Exp), inc="Ssp")
        p.op("scalar", lambda e: e.activation(dtpcol[:, 0:1], ssm_f[:, 64:65], AF.Exp),
             inc="Ssp")
        p.op("scalar", lambda e: e.activation(dtprow, dtprow, AF.Ln, bias=1.0),
             inc="Ssp")
        p.op("scalar", lambda e: e.activation(dtpcol[:, 0:1], dtpcol[:, 0:1], AF.Ln,
                                              bias=1.0), inc="Ssp")
        p.wait("vector", "Ssp")
        p.op("vector", lambda e: e.tensor_tensor_scan(crow, onesr, dtprow, 0.0,
                                                      OP.mult, OP.add), inc="Vsp")
        p.wait("vector", *hist["DxcT"][gc])
        p.wait("vector", "Tdh")
        p.op("vector", lambda e, b=gc % 4, par=par: e.tensor_scalar(
            x2_2[par], xcT4[b], dtpcol[:, 0:1], None, OP.mult), inc="Vx2")
        hist["Vx2"].append(p.c("Vx2"))

    def prep_cT(gc):
        p.wait("tensor", "Vsp")
        p.wait("tensor", "Ses")
        p.wait("tensor", "Vpe")
        p.op("tensor", lambda e: e.matmul(v_pe1, arow16, crow, start=True, stop=True),
             inc="Tpe")
        p.op("tensor", lambda e: e.matmul(v_pe2, crow, neg1r[:, 0:1],
                                          start=True, stop=False), inc="Tpe")
        p.op("tensor", lambda e: e.matmul(v_pe2, onesr, crow[:, 127:128],
                                          start=False, stop=True), inc="Tpe")
        p.wait("tensor", "Vdl")
        p.op("tensor", lambda e: e.matmul(v_dl, onesr, crow, start=True, stop=False),
             inc="Tdl")
        p.op("tensor", lambda e: e.matmul(v_dl, crow, neg1r, start=False, stop=True),
             inc="Tdl")

    def prep_cVS(gc):
        par = gc % 2
        p.wait("vector", "Tpe")
        p.wait("vector", "Tdl")
        p.op("vector", lambda e: e.tensor_copy(stat[:, 8:9], v_pe2), inc="Vpe")
        p.wait("scalar", "Tpe")
        p.wait("scalar", "Vpe")
        p.op("scalar", lambda e: e.activation(es_f, v_pe1, AF.Exp), inc="Ses")
        p.op("scalar", lambda e: e.activation(efac, A_bc, AF.Exp,
                                              scale=stat[:, 8:9]), inc="Ses")
        p.wait("vector", "Ses")
        if gc >= 2:
            p.wait("vector", "Ty", hist["Ty"][gc - 2])
        p.wait("vector", "Tdh")
        p.op("vector", lambda e, par=par: e.tensor_tensor(
            ceT2[par], ssmT[32:48, :], es_f, OP.mult))
        p.op("vector", lambda e, par=par: e.tensor_tensor(
            bex2[par], ssm_f[:, 0:16], efac, OP.mult))
        p.op("vector", lambda e, par=par: e.tensor_copy(
            dcol2[32:48, par:par + 1], es_f[:, 127:128]), inc="Vce")
        hist["Vce"].append(p.c("Vce"))
        p.wait("vector", "Tdl")
        p.op("vector", lambda e: e.tensor_tensor(dl_f, v_dl, maskut, OP.add),
             inc="Vdl")

    def prep_d_cb(gc, h):
        if h == 0:
            p.wait("tensor", "Vpp")
            p.wait("tensor", "Dbc")
        else:
            p.wait("tensor", "Vpp", 2 * gc + 1)
        for si in range(8):
            s_ = h * 8 + si
            p.op("tensor", lambda e, si=si, s_=s_: e.matmul(
                P2[:, si * 128:(si + 1) * 128],
                bcrow[:, s_ * 128:(s_ + 1) * 128],
                bcrow[:, (32 + s_) * 128:(33 + s_) * 128],
                start=True, stop=True),
                 inc=("Tcb" if si == 7 else None))

    def prep_dV(gc):
        gidx, cc = divmod(gc, NCH)
        par = gc % 2
        t0 = cc * T
        dl_rep = dl_f.rearrange("p t -> p () t").broadcast_to([128, 8, 128])
        for h in range(2):
            p.wait("vector", "Sea")
            a_rep = A_bc[:, h * 8:(h + 1) * 8].rearrange(
                "p s -> p s ()").broadcast_to([128, 8, 128])
            p.op("vector", lambda e, a_rep=a_rep: e.tensor_tensor(
                dall.rearrange("p (s t) -> p s t", t=128), dl_rep, a_rep, OP.mult),
                 inc="Vda")
            p.wait("scalar", "Vda")
            p.op("scalar", lambda e: e.activation(eall, dall, AF.Exp), inc="Sea")
            p.wait("vector", "Tcb", 2 * gc + h + 1)
            p.wait("vector", "Sea")
            p.op("vector", lambda e: e.tensor_tensor(
                p_sb, eall, P2[:, 0:1024], OP.mult), inc="Vpp")
            p.op("vector", lambda e: e.tensor_tensor(
                p_sb[:, 0:512], p_sb[:, 0:512], p_sb[:, 512:1024], OP.add))
            p.op("vector", lambda e: e.tensor_tensor(
                p_sb[:, 0:256], p_sb[:, 0:256], p_sb[:, 256:512], OP.add))
            if h == 0:
                if gc >= 2:
                    p.wait("vector", "Ty", hist["Ty"][gc - 2])
                p.op("vector", lambda e, par=par: e.tensor_tensor(
                    m_2[par], p_sb[:, 0:128], p_sb[:, 128:256], OP.add))
            else:
                p.op("vector", lambda e: e.tensor_tensor(
                    p_sb[:, 0:128], p_sb[:, 0:128], p_sb[:, 128:256], OP.add))
                p.op("vector", lambda e, par=par: e.tensor_tensor(
                    m_2[par], m_2[par], p_sb[:, 0:128], OP.add), inc="Vm")
        hist["Vm"].append(p.c("Vm"))
        p.wait("vector", "Scv")
        for m in range(16):
            p.op("vector", lambda e, m=m, t0=t0, par=par: e.tensor_scalar(
                dterm2[par].rearrange("p (a t) -> p a t", t=128)[:, m, :],
                xcs_b[:, m, t0:t0 + T], Dm[:, m:m + 1], None, OP.mult),
                 inc=("VdtX" if m == 15 else None))
        hist["Vdt"].append(p.c("VdtX"))

    def heavy_memset(hc):
        p.wait("vector", "Tdh")
        p.wait("vector", "Ty")
        p.op("vector", lambda e: e.memset(H_sb, 0.0), inc="Vhu")

    def heavy_T1(hc):
        par = hc % 2
        p.wait("tensor", "Vm", hist["Vm"][hc])
        p.wait("tensor", "Vx2", hist["Vx2"][hc])
        p.wait("tensor", "Vce", hist["Vce"][hc])
        p.wait("tensor", "Vhu")
        p.wait("tensor", "Vhs")
        if hc >= 1:
            p.wait("tensor", "Vr", hist["Vr"][hc - 1])
        for h in range(2):
            if h == 0:
                p.wait("tensor", "Vy")
            else:
                p.wait("tensor", "Vy", 2 * hc + 1)
            for m in range(8):
                md = h * 8 + m
                p.op("tensor", lambda e, m=m, md=md, par=par: e.matmul(
                    P3[:, m * 128:(m + 1) * 128],
                    x2_2[par][:, md * 128:(md + 1) * 128],
                    m_2[par], start=True, stop=False))
                p.op("tensor", lambda e, m=m, md=md, par=par: e.matmul(
                    P3[:, m * 128:(m + 1) * 128], H_sb[:, md * 128:(md + 1) * 128],
                    ceT2[par], start=False, stop=True),
                     inc=("Ty" if m == 7 else None))
        hist["Ty"].append(p.c("Ty"))

    def heavy_T2(hc, h):
        par = hc % 2
        if h == 0:
            p.wait("tensor", "Vy", 2 * (hc + 1))
        else:
            p.wait("tensor", "Vhs", 2 * hc + 1)
        for q_ in range(2):
            p.op("tensor", lambda e, h=h, q_=q_, par=par: e.matmul(
                P3[32:48, q_ * 512:(q_ + 1) * 512], bex2[par],
                x2_2[par][:, h * 1024 + q_ * 512:h * 1024 + (q_ + 1) * 512],
                start=True, stop=True), inc=("Tdh" if q_ == 1 else None))

    def heavy_T5(hc):
        p.wait("tensor", "Vyf", hc + 1)
        p.wait("tensor", "Vhs", 2 * (hc + 1))
        for m in range(16):
            for n in range(2):
                p.op("tensor", lambda e, m=m, n=n: e.matmul(
                    P3[:, n * 512:(n + 1) * 512], yT_sb[:, m * 128:(m + 1) * 128],
                    woutT[:, m, n * 512:(n + 1) * 512],
                    start=(m == 0), stop=(m == 15)),
                     inc=("Tmo" if (m == 15 and n == 1) else None))

    def heavy_V(hc):
        gidx, cc = divmod(hc, NCH)
        par = hc % 2
        t0 = cc * T
        p.wait("vector", "Sst")
        p.wait("vector", "Ty", 2 * hc + 1)
        p.op("vector", lambda e, par=par: e.tensor_tensor(
            yT_sb[:, 0:1024], P3[:, 0:1024], dterm2[par][:, 0:1024], OP.add),
             inc="Vy")
        p.wait("vector", "Ty", hist["Ty"][hc])
        p.op("vector", lambda e, par=par: e.tensor_tensor(
            yT_sb[:, 1024:2048], P3[:, 0:1024], dterm2[par][:, 1024:2048], OP.add),
             inc="Vy")
        hist["Vy"].append(p.c("Vy"))
        p.op("vector", lambda e, par=par: e.tensor_scalar(
            H_sb, H_sb, dcol2[32:48, par:par + 1], None, OP.mult), inc="Vhu")
        p.wait("vector", "Tdh", 2 * hc + 1)
        for q_ in range(2):
            p.op("vector", lambda e, q_=q_: e.tensor_tensor(
                H_sb[:, q_ * 512:(q_ + 1) * 512],
                H_sb[:, q_ * 512:(q_ + 1) * 512],
                P3[32:48, q_ * 512:(q_ + 1) * 512], OP.add),
                 inc=("Vhs" if q_ == 1 else None))
        yT3 = yT_sb.rearrange("p (a t) -> p a t", t=T)
        p.wait("vector", "Szv", 16 * (gidx + 1))
        p.op("vector", lambda e, yT3=yT3, t0=t0: e.tensor_tensor(
            yT3, yT3, z_b[:, :, t0:t0 + T], OP.mult), inc="Vyf")
        p.wait("vector", "Tdh", 2 * (hc + 1))
        for q_ in range(2):
            p.op("vector", lambda e, q_=q_: e.tensor_tensor(
                H_sb[:, 1024 + q_ * 512:1024 + (q_ + 1) * 512],
                H_sb[:, 1024 + q_ * 512:1024 + (q_ + 1) * 512],
                P3[32:48, q_ * 512:(q_ + 1) * 512], OP.add),
                 inc=("Vhs" if q_ == 1 else None))

    def heavy_Q(hc):
        gidx, cc = divmod(hc, NCH)
        r, j = divmod(gidx, NG)
        par = hc % 2
        gtok = j * G + cc * T
        p.wait("vector", "Tmo")
        p.wait("vector", *hist["Dxn"][hc])
        p.op("vector", lambda e, b=par: e.tensor_tensor(
            x_c2[b], P3[:, 0:1024], x_c2[b], OP.add), inc="Vr")
        hist["Vr"].append(p.c("Vr"))
        p.wait("scalar", "Vr")
        p.wait("scalar", "Dx1")
        rr = x_c2[par]
        dmp = yT_sb.bitcast(F32)[:, 0:1024]
        p.op("scalar", lambda e, rr=rr, dmp=dmp: e.activation(
            dmp, rr, AF.Identity, accum_out=stat[:, 0:1]), inc="Sst")
        p.op("scalar", lambda e, rr=rr, dmp=dmp: e.activation(
            dmp, rr, AF.Square, accum_out=stat[:, 1:2]), inc="Sst")
        p.wait("vector", "Sst")
        ln_tail(stat, rr, rr)
        p.wait("sync", "Sx1")
        dma("sync", s_x1[r, gtok:gtok + T, :], rr, "Dx1")
        hist["Dx1"].append(p.c("Dx1"))

    p.wait("tensor", "Dw")
    hist["nxT"][1] = dma(
        "gpsimd", xT2[1][:],
        d_xT[0, :, G:2 * G].rearrange("(a p) t -> p a t", p=128), "DxT1")
    for i in range(4):
        wring_load(i)
    p.op("vector", lambda e: e.memset(xi_b[:, :, 0:4], 0.0))
    for i in range(32):
        emit_sec_mm(i)
        if i >= 1:
            emit_sec_evac(i - 1)
    emit_sec_evac(31)
    conv_tail(0)

    for gc in range(64):
        gidx, cc = divmod(gc, NCH)
        hc = gc - 2
        r, j = divmod(gidx, NG)
        if hc >= 0 and hc % 16 == 0:
            heavy_memset(hc)
        if hc >= 0:
            heavy_T1(hc)
            heavy_T2(hc, 0)
            heavy_T2(hc, 1)
        if hc >= 0:
            heavy_V(hc)
        drain_evacs()
        if cc == 0 and gidx + 1 < 16:
            p.wait("vector", "Vcv")
            if (gidx + 1) % NG == 0:
                p.op("vector", lambda e: e.memset(xi_b[:, :, 0:4], 0.0))
            else:
                p.op("vector", lambda e: e.tensor_copy(xi_b[:, :, 0:4],
                                                       xi_b[:, :, 512:516]))
        prep_aT(gc)
        prep_b(gc)
        xi_pair(gidx, cc, 0)
        prep_cT(gc)
        prep_d_cb(gc, 0)
        prep_d_cb(gc, 1)
        if hc >= 0:
            heavy_T5(hc)
        prep_cVS(gc)
        drain_evacs()
        prep_dV(gc)
        if hc >= 0:
            heavy_Q(hc)
        par = gc % 2
        if gc >= 2:
            p.wait("sync", "Vr", hist["Vr"][gc - 2])
            p.wait("sync", "Dx1", hist["Dx1"][gc - 2])
        n = dma("sync", x_c2[par][:], d_x[r, j * G + cc * T:j * G + cc * T + T, :],
                f"Dxn{par}")
        hist["Dxn"].append((f"Dxn{par}", n))
        if cc == 1 and gidx >= 1:
            z_batch(gidx)
        if cc == 2 and gidx + 2 < 16:
            nr, nj = divmod(gidx + 2, NG)
            p.wait("gpsimd", "Tmx", idx_last[gidx] + 1)
            hist["nxT"][gidx + 2] = dma(
                "gpsimd", xT2[gidx % 2][:],
                d_xT[nr, :, nj * G:(nj + 1) * G]
                .rearrange("(a p) t -> p a t", p=128), f"DxT{gidx % 2}")
        xi_pair(gidx, cc, 1)
        if cc == 3 and gidx + 1 < 16:
            drain_evacs()
            conv_tail(gidx + 1)
        if cc != 3:
            wq_i = NWQPRO + gidx * 3 + cc
            if wq_i < len(wq_tiles):
                wq_tile_quant(wq_i)
                if wq_i + 1 < len(wq_tiles):
                    wq_tile_dma(wq_i + 1)

    for hc in (62, 63):
        heavy_T1(hc)
        heavy_T2(hc, 0)
        heavy_T2(hc, 1)
        heavy_V(hc)
        heavy_T5(hc)
        heavy_Q(hc)

    p.wait("scalar", "Tmx")
    p.wait("scalar", "Tmo")
    p.wait("scalar", "Dwq")
    p.wait("scalar", "Dx1")
    p.wait("scalar", "Vpp")
    p.wait("scalar", "Vyf")
    p.wait("sync", "Dx1")
    p.wait("sync", "Vpp")
    p.wait("tensor", "Vpp")
    p.wait("tensor", "Vr")
    p.wait("tensor", "Vxz")
    p.wait("tensor", "Szv")
    p.wait("vector", "Dx1")

    for k in range(8):
        tdma("scalar", wqgT[:, k, :], s_wqg[:, k * 128:(k + 1) * 128], "Dwtgu")
    for k in range(8):
        tdma("scalar", wquT[:, k, :], s_wqu[:, k * 128:(k + 1) * 128], "Dwtgu")
    for k in range(22):
        tdma("scalar", wqdT[:, k, :], s_wqd[:, k * 128:(k + 1) * 128], "Dwtd")

    xq_bf = q_xn.bitcast(BF16)[:, 0:1024]

    bh = {"Vq": [], "Vr2": [], "Dout": [], "DxqT": {}, "DhqT": {}, "Tgu": {}}

    def b0_quant(gq):
        rq, jq = divmod(gq, NG)
        for cc in range(NCH):
            gtok = jq * G + cc * T
            if bh["Vq"]:
                p.wait("sync", "Vq", bh["Vq"][-1])
            dma("sync", x1q, s_x1[rq, gtok:gtok + T, :], "Dx1q")
            need = p.c("Dx1q")
            p.wait("vector", "Dx1q", need)
            p.wait("vector", "Dxq")
            sd = emit_quant(x1q, xq_bf, DM)
            bh["Vq"].append(sd)
            p.wait("sync", "Vq")
            dma("sync", s_xq[rq, gtok:gtok + T, :], xq_bf, "Dxq")

    def b1_xqT(gq, half):
        rq, jq = divmod(gq, NG)
        gh = 2 * gq + half
        p.wait("scalar", "Dxq")
        if (gh - 2) >= 0 and (gh - 2) in bh["Tgu"]:
            p.wait("scalar", "Tgu", bh["Tgu"][gh - 2])
        t0 = jq * G + half * 256
        for k in range(8):
            n = tdma("scalar", xqTh[gh % 2][:, k, :],
                     s_xq[rq, t0:t0 + 256, k * 128:(k + 1) * 128], f"DxqT{gh % 2}")
        bh["DxqT"][gh] = (f"DxqT{gh % 2}", n)

    def b2_gateup(gq, cc):
        gh = 2 * gq + cc // 2
        ct = (cc % 2) * T
        p.wait("tensor", *bh["DxqT"][gh])
        p.wait("tensor", "Dwtgu", 16 * 16)
        p.wait("vector", "Dxq")
        subs = [(0, 1024), (1024, 1024), (2048, 768)]
        for s0, sw in subs:
            p.wait("tensor", "Vgu")
            p.wait("tensor", "Sgu")
            nsub = (sw + 511) // 512
            for k in range(8):
                for n_ in range(nsub):
                    nw = min(512, sw - n_ * 512)
                    p.op("tensor", lambda e, k=k, n_=n_, nw=nw, s0=s0, b=gh % 2, ct=ct:
                         e.matmul(P1[:, n_ * 512:n_ * 512 + nw],
                                  xqTh[b][:, k, ct:ct + T],
                                  wqgT[:, k, s0 + n_ * 512:s0 + n_ * 512 + nw],
                                  start=(k == 0), stop=(k == 7)))
                for n_ in range(nsub):
                    nw = min(512, sw - n_ * 512)
                    p.op("tensor", lambda e, k=k, n_=n_, nw=nw, s0=s0, b=gh % 2, ct=ct:
                         e.matmul(P2[:, n_ * 512:n_ * 512 + nw],
                                  xqTh[b][:, k, ct:ct + T],
                                  wquT[:, k, s0 + n_ * 512:s0 + n_ * 512 + nw],
                                  start=(k == 0), stop=(k == 7)),
                         inc=("Tgu" if (k == 7 and n_ == nsub - 1) else None))
            p.wait("scalar", "Tgu")
            p.wait("scalar", "Vgu")
            p.op("scalar", lambda e, sw=sw: e.activation(sg_sb[:, 0:sw], P1[:, 0:sw],
                                                         AF.Silu), inc="Sgu")
            p.wait("vector", "Sgu")
            p.wait("vector", "Dhq")
            p.op("vector", lambda e, sw=sw: e.tensor_tensor(
                sg_sb[:, 0:sw], sg_sb[:, 0:sw], P2[:, 0:sw], OP.mult), inc="Vgu")
            emit_quant(sg_sb[:, 0:sw], hq_sub[:, 0:sw], sw)
            p.wait("sync", "Vq")
            if (gq - 2) in bh["DhqT"]:
                p.wait("sync", "DhqT", bh["DhqT"][gq - 2])
            dma("sync", s_hq[gq % 2, cc * T:(cc + 1) * T, s0:s0 + sw],
                hq_sub[:, 0:sw], "Dhq")
        bh["Tgu"][gh] = p.c("Tgu")

    def b3_hqT(gq):
        p.wait("scalar", "Dhq")
        p.wait("scalar", "Tdn")
        for k in range(22):
            tdma("scalar", hqT_b[:, k, :],
                 s_hq[gq % 2, :, k * 128:(k + 1) * 128], "DhqT")
        bh["DhqT"][gq] = p.c("DhqT")

    def b4_down(gq, cc):
        rq, jq = divmod(gq, NG)
        gtok = jq * G + cc * T
        p.wait("sync", "Sx1")
        dma("sync", x1l, s_x1[rq, gtok:gtok + T, :], "Dx1b")
        need = p.c("Dx1b")
        p.wait("tensor", "DhqT", bh["DhqT"][gq])
        p.wait("tensor", "Dwtd", 22 * 16)
        p.wait("tensor", "Vr2")
        for k in range(22):
            for n_ in range(2):
                p.op("tensor", lambda e, k=k, n_=n_, cc=cc: e.matmul(
                    P3[:, n_ * 512:(n_ + 1) * 512], hqT_b[:, k, cc * T:(cc + 1) * T],
                    wqdT[:, k, n_ * 512:(n_ + 1) * 512],
                    start=(k == 0), stop=(k == 21)),
                     inc=("Tdn" if (k == 21 and n_ == 1) else None))
        p.wait("vector", "Tdn")
        p.wait("vector", "Dx1b", need)
        p.wait("vector", "Sx1")
        p.op("vector", lambda e: e.tensor_tensor(x1l, P3[:, 0:1024], x1l, OP.add),
             inc="Vr2")
        bh["Vr2"].append(p.c("Vr2"))
        p.wait("scalar", "Vr2")
        if bh["Dout"]:
            p.wait("scalar", "Dout", bh["Dout"][-1])
        p.op("scalar", lambda e: e.activation(out_sb, x1l, AF.Identity,
                                              accum_out=statB[:, 0:1]), inc="Sst")
        p.op("scalar", lambda e: e.activation(out_sb, x1l, AF.Square,
                                              accum_out=statB[:, 1:2]), inc="Sst")
        p.wait("vector", "Sst")
        ln_tail(statB, x1l, out_sb)
        p.wait("sync", "Sx1")
        dma("sync", d_out[rq, gtok:gtok + T, :], out_sb, "Dout")
        bh["Dout"].append(p.c("Dout"))

    NGQ = R * NG
    b0_quant(0)
    b1_xqT(0, 0)
    b1_xqT(0, 1)
    for gq in range(NGQ):
        if gq + 1 < NGQ:
            b0_quant(gq + 1)
        for cc in range(NCH):
            b2_gateup(gq, cc)
            if cc == 1 and gq + 1 < NGQ:
                b1_xqT(gq + 1, 0)
            if cc == 3 and gq + 1 < NGQ:
                b1_xqT(gq + 1, 1)
            if gq >= 1:
                b4_down(gq - 1, cc)
        b3_hqT(gq)
    for cc in range(NCH):
        b4_down(NGQ - 1, cc)

    p.wait("sync", "Dout")

    with nc.Block() as block:
        @block.tensor
        def _(eng):
            for fn in p.streams["tensor"]:
                fn(eng)

        @block.vector
        def _(eng):
            for fn in p.streams["vector"]:
                fn(eng)

        @block.scalar
        def _(eng):
            for fn in p.streams["scalar"]:
                fn(eng)

        @block.gpsimd
        def _(eng):
            for fn in p.streams["gpsimd"]:
                fn(eng)

        @block.sync
        def _(eng):
            for fn in p.streams["sync"]:
                fn(eng)

    return nc


def kernel(L0_attn_x, L0_attn_W_in, L0_attn_conv_w, L0_attn_conv_b, L0_attn_W_x,
           L0_attn_A_log, L0_attn_D, L0_attn_W_out, L0_ffn_w_gate, L0_ffn_w_up,
           L0_ffn_w_down):
    x = np.ascontiguousarray(L0_attn_x, dtype=np.float32)
    winT = np.ascontiguousarray(np.asarray(L0_attn_W_in, np.float32).T)
    wxt_raw = np.asarray(L0_attn_W_x, np.float32).T
    wxT = np.zeros((DI, 72), np.float32)
    wxT[:, 0:16] = wxt_raw[:, 0:16]
    wxT[:, 32:48] = wxt_raw[:, 16:32]
    wxT[:, 64] = wxt_raw[:, 32]
    woutT = np.ascontiguousarray(np.asarray(L0_attn_W_out, np.float32).T)
    cw = np.ascontiguousarray(
        np.asarray(L0_attn_conv_w, np.float32)[:, 0, :].reshape(16, 128, 4)
        .transpose(1, 0, 2))
    cb = np.ascontiguousarray(np.asarray(L0_attn_conv_b, np.float32)
                              .reshape(16, 128).T)
    Dmh = np.ascontiguousarray(np.asarray(L0_attn_D, np.float32).reshape(16, 128).T)
    alog = np.ascontiguousarray(L0_attn_A_log, np.float32)
    wg = np.ascontiguousarray(L0_ffn_w_gate, np.float32)
    wu = np.ascontiguousarray(L0_ffn_w_up, np.float32)
    wd = np.ascontiguousarray(L0_ffn_w_down, np.float32)

    nc = build_program()
    in_maps = []
    for i in range(8):
        xs = np.ascontiguousarray(x[4 * i:4 * i + 4])
        xTs = np.ascontiguousarray(xs.transpose(0, 2, 1))
        in_maps.append({
            "x": xs, "xT": xTs, "winT": winT, "wxT": wxT, "woutT": woutT,
            "cw": cw, "cb": cb, "Dm": Dmh, "alog": alog,
            "wg": wg, "wu": wu, "wd": wd,
        })
    res = run_bass_kernel_spmd(nc, in_maps, core_ids=list(range(8)))
    return np.concatenate([r["out"] for r in res.results], axis=0)


# version 12
# speedup vs baseline: 1.1045x; 1.1045x over previous
import numpy as np
import concourse.bass as bass
import concourse.mybir as mybir
from concourse.bass_utils import run_bass_kernel_spmd

F32 = mybir.dt.float32
BF16 = mybir.dt.bfloat16
AF = mybir.ActivationFunctionType
OP = mybir.AluOpType
AX = mybir.AxisListType

R = 4
L = 2048
DM = 1024
DI = 2048
S = 16
NI = 2816
T = 128
G = 512
NG = L // G
NCH = G // T
MASKV = 1.0e30
DBG_OFF = {}


class P:
    ENGS = ("tensor", "vector", "scalar", "gpsimd", "sync")

    def __init__(self, nc):
        self.nc = nc
        self.streams = {e: [] for e in self.ENGS}
        self.sems = {}
        self.cnt = {}
        self.last = {e: None for e in self.ENGS}

    def sem(self, name):
        if name not in self.sems:
            self.sems[name] = self.nc.alloc_semaphore(name)
            self.cnt[name] = 0
        return self.sems[name]

    def op(self, eng, fn, inc=None, dinc=None):
        if eng != "tensor" and self.last[eng] is not None:
            ls, lv = self.last[eng]
            h0 = self.sem(ls)
            self.streams[eng].append(lambda e, h0=h0, v=lv: e.wait_ge(h0, v))
        if dinc is not None:
            h = self.sem(dinc)
            self.cnt[dinc] += 16
            self.streams[eng].append(lambda e, fn=fn, h=h: fn(e).then_inc(h, 16))
            return self.cnt[dinc]
        sname = inc if inc is not None else "E" + eng
        h = self.sem(sname)
        self.cnt[sname] += 1
        self.streams[eng].append(lambda e, fn=fn, h=h: fn(e).then_inc(h, 1))
        self.last[eng] = (sname, self.cnt[sname])
        return self.cnt[sname]

    def wait(self, eng, sname, val=None):
        if val is None:
            val = self.cnt.get(sname, 0)
        if val <= 0:
            return
        h = self.sem(sname)
        self.streams[eng].append(lambda e, h=h, v=val: e.wait_ge(h, v))

    def c(self, sname):
        return self.cnt.get(sname, 0)


def build_program():
    nc = bass.Bass("TRN2", target_bir_lowering=False)
    p = P(nc)

    d_x = nc.dram_tensor("x", [R, L, DM], F32, kind="ExternalInput").ap()
    d_xT = nc.dram_tensor("xT", [R, DM, L], F32, kind="ExternalInput").ap()
    d_winT = nc.dram_tensor("winT", [DM, 2 * DI], F32, kind="ExternalInput").ap()
    d_wxT = nc.dram_tensor("wxT", [DI, 72], F32, kind="ExternalInput").ap()
    d_woutT = nc.dram_tensor("woutT", [DI, DM], F32, kind="ExternalInput").ap()
    d_cw = nc.dram_tensor("cw", [128, 16, 4], F32, kind="ExternalInput").ap()
    d_cb = nc.dram_tensor("cb", [128, 16], F32, kind="ExternalInput").ap()
    d_Dm = nc.dram_tensor("Dm", [128, 16], F32, kind="ExternalInput").ap()
    d_alog = nc.dram_tensor("alog", [DI, S], F32, kind="ExternalInput").ap()
    d_wg = nc.dram_tensor("wg", [NI, DM], F32, kind="ExternalInput").ap()
    d_wu = nc.dram_tensor("wu", [NI, DM], F32, kind="ExternalInput").ap()
    d_wd = nc.dram_tensor("wd", [DM, NI], F32, kind="ExternalInput").ap()
    d_out = nc.dram_tensor("out", [R, L, DM], F32, kind="ExternalOutput").ap()
    s_winTb = nc.dram_tensor("s_winTb", [DM, 2 * DI], BF16).ap()
    s_xc = nc.dram_tensor("s_xc", [R, DI, L], BF16).ap()
    s_x1 = nc.dram_tensor("s_x1", [R, L, DM], F32).ap()
    s_xq = nc.dram_tensor("s_xq", [R, L, DM], BF16).ap()
    s_wqg = nc.dram_tensor("s_wqg", [NI, DM], BF16).ap()
    s_wqu = nc.dram_tensor("s_wqu", [NI, DM], BF16).ap()
    s_wqd = nc.dram_tensor("s_wqd", [DM, NI], BF16).ap()
    s_hq = nc.dram_tensor("s_hq", [2, G, NI], BF16).ap()

    import ml_dtypes as _mld
    mask_np = np.where(np.arange(T)[:, None] > np.arange(T)[None, :],
                       np.float32(MASKV), np.float32(0.0)).astype(_mld.bfloat16)
    d_mask = nc.inline_tensor(mask_np, "maskut").ap()
    d_ones = nc.inline_tensor(np.ones((1, 128), np.float32), "onesr").ap()
    d_neg1 = nc.inline_tensor(-np.ones((1, 128), np.float32), "negr").ap()
    d_id = nc.inline_tensor(np.eye(128, dtype=np.float32), "idf").ap()

    ARENA_B = 209920
    arena = nc.alloc_sbuf_tensor("arena", [128, ARENA_B // 2], BF16)
    arena_ap = arena.ap()
    cur = [0]

    def reg(nbytes, shape, dtype, at=None):
        import traceback as _tb
        nbytes = (nbytes + 31) // 32 * 32
        off = cur[0] if at is None else at
        DBG_OFF.setdefault(_tb.extract_stack()[-2].lineno, []).append(off)
        if at is None:
            cur[0] += nbytes
            assert cur[0] <= ARENA_B, f"arena overflow {cur[0]}"
        n = int(np.prod(shape[1:]))
        esz = 4 if dtype == F32 else 2
        assert n * esz <= nbytes
        ap = arena_ap[:shape[0], off // 2:(off + ((n * esz + 3) // 4) * 4) // 2]
        if dtype == F32:
            ap = ap.bitcast(F32)
        ap = ap[:, 0:n]
        if len(shape) > 2:
            dims = " ".join(f"d{i}" for i in range(1, len(shape)))
            ap = ap.rearrange(f"p ({dims}) -> p {dims}",
                              **{f"d{i}": shape[i] for i in range(1, len(shape))})
        return ap

    q_xn = reg(4096, [128, 1024], F32)
    q_c1 = reg(2048, [128, 1024], BF16)
    q_c2 = reg(2048, [128, 1024], BF16)
    q_c3 = reg(2048, [128, 1024], BF16)
    q_sgn = reg(2048, [128, 1024], BF16)
    bm_f = reg(256, [128, 64], F32)
    sc_f = reg(256, [128, 64], F32)
    scb = reg(128, [128, 64], BF16)
    inv_f = reg(256, [128, 64], F32)
    wraw = reg(1024 * 4, [128, 1024], F32)
    wq_t = reg(1024 * 2, [128, 1024], BF16)

    ab_base = cur[0]

    NRING = 6
    wring = [reg(8 * 128 * 2, [128, 8, 128], BF16) for _ in range(NRING)]
    woutT = reg(16 * 1024 * 2, [128, 16, 1024], BF16)
    xT2 = [reg(8 * 512 * 2, [128, 8, 512], BF16) for _ in range(2)]
    xi_b = reg(16 * 516 * 2, [128, 16, 516], BF16)
    xcs_b = reg(16 * 512 * 2, [128, 16, 512], BF16)
    z_b = reg(16 * 512 * 2, [128, 16, 512], BF16)
    xcT4 = [reg(DI * 2, [128, DI], BF16) for _ in range(4)]
    x2_2 = [reg(DI * 2, [128, DI], BF16) for _ in range(2)]
    dall = reg(1024 * 4, [128, 1024], F32)
    eall = reg(1024 * 2, [128, 1024], BF16)
    p_sb = reg(1024 * 2, [128, 1024], BF16)
    dterm2 = [reg(2048 * 2, [128, 2048], BF16) for _ in range(2)]
    yT_sb = reg(2048 * 2, [128, 2048], BF16)
    x_c2 = [reg(4096, [128, 1024], F32) for _ in range(2)]
    bcrow_t = reg(12288, [128, 6144], BF16)
    bcrow = bcrow_t[0:1, :]
    wxT = reg(16 * 72 * 2, [128, 16, 72], BF16)
    cw = reg(256, [128, 16, 4], F32)
    cb = reg(64, [128, 16], F32)
    Dm = reg(64, [128, 16], F32)
    A_bc = reg(64, [128, 16], F32)
    maskut = reg(256, [128, 128], BF16)
    onesr_t = reg(512, [128, 128], F32)
    onesr = onesr_t[0:1, :]
    neg1r_t = reg(512, [128, 128], F32)
    neg1r = neg1r_t[0:1, :]
    idf = reg(512, [128, 128], F32)
    arow_t = reg(64, [128, 16], F32)
    arow16 = arow_t[0:1, :]
    tiny = reg(128, [128, 32], F32)
    H_sb = reg(DI * 2, [128, DI], BF16)[32:32 + S, :]
    ssmT_t = reg(512, [128, 128], F32)
    ssmT = ssmT_t[0:72, :]
    ssm_f = reg(288, [128, 72], F32)
    es_f = reg(512, [128, 128], F32)[32:48, :]
    efac = reg(64, [128, 16], F32)
    ceT2 = [reg(256, [128, 128], BF16)[32:48, :] for _ in range(2)]
    bex2 = [reg(64, [128, 16], BF16) for _ in range(2)]
    dcol2 = reg(32, [128, 8], F32)
    dl_f = reg(512, [128, 128], F32)
    m_2 = [reg(256, [128, 128], BF16) for _ in range(2)]
    crow_t = reg(512, [128, 128], F32)
    crow = crow_t[0:1, :]
    dtprow_t = reg(512, [128, 128], F32)
    dtprow = dtprow_t[0:1, :]
    dtraw_t = reg(512, [128, 128], F32)
    dtraw = dtraw_t[0:1, :]
    dtpcol = reg(32, [128, 8], F32)
    stat = reg(64, [128, 16], F32)
    a_end = cur[0]
    assert a_end <= ARENA_B, f"A {a_end}"

    cur[0] = ab_base
    wqgT = reg(8 * 2816 * 2, [128, 8, 2816], BF16)
    wquT = reg(8 * 2816 * 2, [128, 8, 2816], BF16)
    wqdT = reg(22 * 1024 * 2, [128, 22, 1024], BF16)
    xqTh = [reg(8 * 256 * 2, [128, 8, 256], BF16) for _ in range(2)]
    hqT_b = reg(22 * 512 * 2, [128, 22, 512], BF16)
    sg_sb = reg(4096, [128, 1024], F32)
    hq_sub = reg(2048, [128, 1024], BF16)
    x1l = reg(4096, [128, 1024], F32)
    x1q = reg(4096, [128, 1024], F32)
    out_sb = reg(4096, [128, 1024], F32)
    statB = reg(64, [128, 16], F32)
    b_end = cur[0]
    assert b_end <= ARENA_B, f"B {b_end}"
    xq_out = arena_ap[:, q_xn.stride() and 0:0]

    P1 = nc.alloc_psum_tensor("Ps1", [128, 1024], F32).ap()
    P2 = nc.alloc_psum_tensor("Ps2", [128, 1024], F32).ap()
    P3 = nc.alloc_psum_tensor("Ps3", [128, 1024], F32).ap()
    P4 = nc.alloc_psum_tensor("Ps4", [128, 512], F32).ap()
    P5 = nc.alloc_psum_tensor("Ps5", [128, 512], F32).ap()
    v_dl = P4[:, 0:128]
    v_ssmT = P4[0:72, 128:256]
    v_pe1 = P4[32:48, 256:384]
    v_pe2 = P4[:, 384:385]
    v_tr = P5[:, 0:128]

    def dma(qeng, dst, src, dname):
        return p.op(qeng, lambda e, d=dst, s=src: e.dma_start(d, s), dinc=dname)

    def tdma(qeng, dst, src, dname):
        return p.op(qeng, lambda e, d=dst, s=src:
                    e.dma_start(d, s, transpose=True), dinc=dname)

    dma("gpsimd", s_winTb, d_winT, "Dwtb")
    dma("gpsimd", wxT, d_wxT.rearrange("(a p) b -> p a b", p=128), "Dw")
    dma("gpsimd", woutT, d_woutT.rearrange("(a p) b -> p a b", p=128), "Dw")
    dma("sync", cw, d_cw, "Dc")
    dma("sync", cb, d_cb, "Dc")
    dma("sync", Dm, d_Dm, "Dc")
    dma("sync", tiny[0:1, 0:16], d_alog[0:1, :], "Dc")
    dma("sync", maskut, d_mask, "Dc")
    dma("sync", onesr, d_ones, "Dc")
    dma("sync", neg1r, d_neg1, "Dc")
    dma("sync", idf, d_id, "Dc")
    dma("gpsimd", xT2[0][:],
        d_xT[0, :, 0:G].rearrange("(a p) t -> p a t", p=128), "DxT0")

    p.wait("scalar", "Dc")
    p.op("scalar", lambda e: e.activation(tiny[0:1, 16:32], tiny[0:1, 0:16], AF.Exp),
         inc="Sc")
    p.wait("vector", "Sc")
    p.op("vector", lambda e: e.tensor_scalar(arow16[:, 0:16], tiny[0:1, 16:32],
                                             -1.0, None, OP.mult), inc="Vc")
    p.wait("tensor", "Vc")
    p.wait("tensor", "Dc")
    p.op("tensor", lambda e: e.matmul(P4[:, 0:16], onesr, arow16[:, 0:16],
                                      start=True, stop=True), inc="Tc")
    p.wait("vector", "Tc")
    p.op("vector", lambda e: e.tensor_copy(A_bc, P4[:, 0:16]), inc="Vc")

    def emit_quant(src_f32, dst_bf16, width):
        nb = width // 16
        p.op("vector", lambda e: e.tensor_reduce(
            bm_f[:, 0:nb], src_f32.rearrange("p (b k) -> p b k", k=16),
            AX.X, OP.max, True), inc="Vq")
        p.op("vector", lambda e: e.tensor_scalar(
            sc_f[:, 0:nb], bm_f[:, 0:nb], 1.0 / 6.0, 1e-12, OP.mult, OP.max),
             inc="Vq")
        p.wait("vector", "Vq")
        p.op("vector", lambda e: e.reciprocal(inv_f[:, 0:nb], sc_f[:, 0:nb]),
             inc="Vq")
        p.wait("vector", "Vq")
        p.op("vector", lambda e: e.tensor_copy(scb[:, 0:nb], sc_f[:, 0:nb]))
        inv_bc = inv_f[:, 0:nb].rearrange("p b -> p b ()").broadcast_to([128, nb, 16])
        src_done = p.op("vector", lambda e: e.tensor_tensor(
            q_xn[:, 0:width].rearrange("p (b k) -> p b k", k=16),
            src_f32.rearrange("p (b k) -> p b k", k=16), inv_bc, OP.mult),
             inc="Vq")
        p.wait("scalar", "Vq")
        p.op("scalar", lambda e: e.activation(q_sgn[:, 0:width], q_xn[:, 0:width],
                                              AF.Sign), inc="Sq")
        p.op("scalar", lambda e: e.activation(q_xn[:, 0:width], q_xn[:, 0:width],
                                              AF.Abs), inc="Sq")
        p.wait("vector", "Sq")
        aw = q_xn[:, 0:width]
        v = p.op
        v("vector", lambda e: e.tensor_scalar(q_c1[:, 0:width], aw, 0.25, None, OP.is_gt))
        v("vector", lambda e: e.tensor_scalar(q_c2[:, 0:width], aw, 0.75, None, OP.is_gt))
        v("vector", lambda e: e.tensor_tensor(q_c1[:, 0:width], q_c1[:, 0:width],
                                              q_c2[:, 0:width], OP.add))
        v("vector", lambda e: e.tensor_scalar(q_c2[:, 0:width], aw, 1.25, None, OP.is_gt))
        v("vector", lambda e: e.tensor_scalar(q_c3[:, 0:width], aw, 1.75, None, OP.is_gt))
        v("vector", lambda e: e.tensor_tensor(q_c2[:, 0:width], q_c2[:, 0:width],
                                              q_c3[:, 0:width], OP.add))
        v("vector", lambda e: e.tensor_tensor(q_c1[:, 0:width], q_c1[:, 0:width],
                                              q_c2[:, 0:width], OP.add))
        v("vector", lambda e: e.tensor_scalar(q_c2[:, 0:width], aw, 2.5, None, OP.is_gt))
        v("vector", lambda e: e.tensor_scalar(q_c3[:, 0:width], aw, 3.5, None, OP.is_gt))
        v("vector", lambda e: e.tensor_tensor(q_c2[:, 0:width], q_c2[:, 0:width],
                                              q_c3[:, 0:width], OP.add))
        v("vector", lambda e: e.tensor_scalar(q_c3[:, 0:width], aw, 5.0, None, OP.is_gt))
        v("vector", lambda e: e.scalar_tensor_tensor(q_c3[:, 0:width], q_c3[:, 0:width],
                                                     2.0, q_c2[:, 0:width],
                                                     OP.mult, OP.add))
        v("vector", lambda e: e.scalar_tensor_tensor(q_c1[:, 0:width], q_c1[:, 0:width],
                                                     0.5, q_c3[:, 0:width],
                                                     OP.mult, OP.add))
        v("vector", lambda e: e.tensor_tensor(q_c1[:, 0:width], q_c1[:, 0:width],
                                              q_sgn[:, 0:width], OP.mult))
        scb_bc = scb[:, 0:nb].rearrange("p b -> p b ()").broadcast_to([128, nb, 16])
        v("vector", lambda e: e.tensor_tensor(
            dst_bf16.rearrange("p (b k) -> p b k", k=16),
            q_c1[:, 0:width].rearrange("p (b k) -> p b k", k=16),
            scb_bc, OP.mult), inc="Vq")
        return src_done

    wq_tiles = []
    for dsrc, ddst, ntile, width in ((d_wg, s_wqg, NI // 128, DM),
                                     (d_wu, s_wqu, NI // 128, DM),
                                     (d_wd, s_wqd, DM // 128, NI)):
        nh = (width + 1023) // 1024
        for t_ in range(ntile):
            for h_ in range(nh):
                w0 = h_ * 1024
                ww = min(1024, width - w0)
                wq_tiles.append((dsrc, ddst, t_, w0, ww))

    def wq_tile_dma(idx):
        dsrc, ddst, t_, w0, ww = wq_tiles[idx]
        p.wait("gpsimd", "Vq")
        dma("gpsimd", wraw[:, 0:ww],
            dsrc[t_ * 128:(t_ + 1) * 128, w0:w0 + ww], "Dwr")

    def wq_tile_quant(idx):
        dsrc, ddst, t_, w0, ww = wq_tiles[idx]
        p.wait("vector", "Dwr", 16 * (idx + 1))
        p.wait("vector", "Dwq")
        emit_quant(wraw[:, 0:ww], wq_t[:, 0:ww], ww)
        p.wait("sync", "Vq")
        dma("sync", ddst[t_ * 128:(t_ + 1) * 128, w0:w0 + ww],
            wq_t[:, 0:ww], "Dwq")

    NWQPRO = len(wq_tiles) - 48
    wq_tile_dma(0)
    for i_ in range(NWQPRO):
        wq_tile_quant(i_)
        if i_ + 1 < len(wq_tiles):
            wq_tile_dma(i_ + 1)

    hist = {"Vr": [], "Vx2": [], "DxcT": [], "Dxn": [], "Ty": [], "Dx1": [],
            "Vm": [], "Vce": [], "Vdt": [], "Vy": [],
            "nxT": {0: 16}, "nwg": {}}

    def ln_tail(st, rin, xout):
        v = p.op
        v("vector", lambda e: e.tensor_scalar(st[:, 2:3], st[:, 0:1],
                                              1.0 / 1024, None, OP.mult))
        v("vector", lambda e: e.tensor_scalar(st[:, 3:4], st[:, 1:2],
                                              1.0 / 1024, None, OP.mult))
        v("vector", lambda e: e.tensor_scalar(st[:, 4:5], st[:, 2:3],
                                              st[:, 2:3], None, OP.mult))
        v("vector", lambda e: e.tensor_tensor(st[:, 4:5], st[:, 3:4],
                                              st[:, 4:5], OP.subtract))
        v("vector", lambda e: e.tensor_scalar(st[:, 4:5], st[:, 4:5],
                                              1e-5, None, OP.add))
        v("vector", lambda e: e.memset(st[:, 5:6], 1.0))
        for _ in range(5):
            v("vector", lambda e: e.tensor_scalar(st[:, 6:7], st[:, 5:6],
                                                  st[:, 5:6], None, OP.mult))
            v("vector", lambda e: e.tensor_tensor(st[:, 6:7], st[:, 4:5],
                                                  st[:, 6:7], OP.mult))
            v("vector", lambda e: e.tensor_scalar(st[:, 6:7], st[:, 6:7],
                                                  -0.5, 1.5, OP.mult, OP.add))
            v("vector", lambda e: e.tensor_tensor(st[:, 5:6], st[:, 5:6],
                                                  st[:, 6:7], OP.mult))
        v("vector", lambda e: e.tensor_scalar(st[:, 7:8], st[:, 2:3],
                                              st[:, 5:6], -1.0, OP.mult, OP.mult),
          inc="Vln")
        p.wait("scalar", "Vln")
        p.op("scalar", lambda e: e.activation(xout, rin, AF.Identity,
                                              scale=st[:, 5:6], bias=st[:, 7:8]),
             inc="Sx1")

    def wring_load(gi, m):
        gm = gi * 32 + m
        if gm >= 8:
            p.wait("scalar", "Tmx", gm - 8 + 1)
        if gm == 0:
            p.wait("scalar", "Dwtb")
        hist["nwg"][gm] = dma(
            "scalar", wring[m % 8][:],
            s_winTb[:, m * 128:(m + 1) * 128].rearrange("(a p) c -> p a c", p=128),
            f"Dwg{m % 8}")

    plan = [(0, m) for m in range(32)]
    for g_ in range(16):
        for cc_ in range(4):
            a_ = [(g_ + 1, 4 * cc_ + j) for j in range(4)] if g_ + 1 < 16 else []
            plan += a_[:2]
            if cc_ == 1 and g_ >= 1:
                plan += [(g_, 16 + k) for k in range(16)]
            plan += a_[2:]
    plan_idx = {sec: i for i, sec in enumerate(plan)}
    idx_last = {}
    for i, (gi_, m_) in enumerate(plan):
        idx_last[gi_] = i

    xz_evac_last = [None, None]
    sec_state = {}
    evac_pend = []

    def wring_load(i):
        gi, m = plan[i]
        qe = "scalar" if (i % NRING) % 2 == 0 else "gpsimd"
        if i >= NRING:
            p.wait(qe, "Tmx", i - NRING + 1)
        if i <= 1:
            p.wait(qe, "Dwtb")
        hist["nwg"][i] = dma(
            qe, wring[i % NRING][:],
            s_winTb[:, m * 128:(m + 1) * 128].rearrange("(a p) c -> p a c", p=128),
            f"Dwg{i % NRING}")

    def emit_sec_mm(i):
        gi, m = plan[i]
        half = i % 2
        pdst = P1[:, half * 512:(half + 1) * 512]
        if xz_evac_last[half] is not None:
            p.wait("tensor", *xz_evac_last[half])
        p.wait("tensor", f"Dwg{i % NRING}", hist["nwg"][i])
        p.wait("tensor", f"DxT{gi % 2}", hist["nxT"][gi])
        for k in range(8):
            p.op("tensor", lambda e, k=k, i=i, pdst=pdst, buf=gi % 2:
                 e.matmul(pdst, wring[i % NRING][:, k, :], xT2[buf][:, k, :],
                          start=(k == 0), stop=(k == 7)),
                 inc=("Tmx" if k == 7 else None))
        sec_state[i] = (half, p.c("Tmx"))
        if i + 4 < len(plan):
            wring_load(i + 4)

    def emit_sec_evac(i):
        gi, m = plan[i]
        half, tmx = sec_state.pop(i)
        pdst = P1[:, half * 512:(half + 1) * 512]
        if m < 16:
            p.wait("scalar", "Tmx", tmx)
            cn = p.op("scalar", lambda e, m=m, pdst=pdst:
                      e.activation(xi_b[:, m, 4:516], pdst, AF.Identity),
                      inc="Vxz")
            xz_evac_last[half] = ("Vxz", cn)
        else:
            p.wait("scalar", "Tmx", tmx)
            if m == 16:
                p.wait("scalar", "Vyf", 4 * gi)
            cn = p.op("scalar", lambda e, m=m, pdst=pdst:
                      e.activation(z_b[:, m - 16, :], pdst, AF.Silu), inc="Szv")
            xz_evac_last[half] = ("Szv", cn)

    def drain_evacs():
        while evac_pend:
            emit_sec_evac(evac_pend.pop(0))

    def xi_pair(g, cc, which):
        if g + 1 >= 16:
            return
        for j in (0, 1):
            i = plan_idx[(g + 1, 4 * cc + 2 * which + j)]
            emit_sec_mm(i)
            evac_pend.append(i)

    def z_batch(g):
        base = plan_idx[(g, 16)]
        for k in range(16):
            emit_sec_mm(base + k)
            if k >= 1:
                emit_sec_evac(base + k - 1)
        emit_sec_evac(base + 15)

    def conv_tail(gidx):
        r, j = divmod(gidx, NG)
        p.wait("vector", "Tsm")
        p.wait("vector", "Dxc")
        p.wait("vector", "Vxz")
        for g in range(16):
            p.op("vector", lambda e, g=g: e.tensor_scalar(
                xcs_b[:, g, :], xi_b[:, g, 1:513], cw[:, g, 0:1], cb[:, g:g + 1],
                OP.mult, OP.add))
            for tap in range(1, 4):
                inc = "Vcv" if (tap == 3 and g == 15) else None
                p.op("vector", lambda e, g=g, tap=tap: e.scalar_tensor_tensor(
                    xcs_b[:, g, :], xi_b[:, g, 1 + tap:513 + tap],
                    cw[:, g, tap:tap + 1], xcs_b[:, g, :], OP.mult, OP.add), inc=inc)
        p.wait("scalar", "Vcv")
        p.op("scalar", lambda e: e.activation(
            xcs_b.rearrange("p a t -> p (a t)"),
            xcs_b.rearrange("p a t -> p (a t)"), AF.Silu), inc="Scv")
        p.wait("sync", "Scv")
        dma("sync", s_xc[r, :, j * G:(j + 1) * G].rearrange("(a p) t -> p a t", p=128),
            xcs_b, "Dxc")
        p.wait("sync", "Dxc")
        for cc in range(NCH):
            gc = gidx * NCH + cc
            if gc >= 4:
                p.wait("sync", "Vx2", hist["Vx2"][gc - 4])
            gtok = j * G + cc * T
            n = tdma("sync", xcT4[gc % 4][:], s_xc[r, :, gtok:gtok + T],
                     f"DxcT{gc % 4}")
            hist["DxcT"].append((f"DxcT{gc % 4}", n))

    def prep_aT(gc):
        gidx, cc = divmod(gc, NCH)
        t0 = cc * T
        p.wait("tensor", "Vdl")
        p.wait("tensor", "Scv")
        p.wait("tensor", "Ses")
        p.wait("tensor", "Vpe")
        for k in range(16):
            p.op("tensor", lambda e, k=k, t0=t0: e.matmul(
                v_ssmT, wxT[:, k, :], xcs_b[:, k, t0:t0 + T],
                start=(k == 0), stop=(k == 15)), inc=("Tsm" if k == 15 else None))

    def prep_b(gc):
        par = gc % 2
        p.wait("vector", "Tsm")
        p.op("vector", lambda e: e.tensor_copy(ssmT, v_ssmT), inc="Vsm")
        p.wait("gpsimd", "Vsm")
        p.wait("gpsimd", "Tcb")
        p.wait("gpsimd", "Vsp")
        p.wait("gpsimd", "Ssp")
        dma("gpsimd", bcrow.rearrange("p (a c) -> p a c", c=128),
            ssmT_t[0:48, :], "Dbc")
        dma("gpsimd", dtraw, ssmT_t[64:65, :], "Dbc")
        p.wait("tensor", "Vsm")
        p.op("tensor", lambda e: e.transpose(v_tr, ssmT_t, idf), inc="Tst")
        p.wait("vector", "Tst")
        p.op("vector", lambda e: e.tensor_copy(ssm_f[:, 0:72], v_tr[:, 0:72]),
             inc="Vsm")
        p.wait("scalar", "Vsm")
        p.wait("scalar", "Dbc")
        p.op("scalar", lambda e: e.activation(dtprow, dtraw, AF.Exp), inc="Ssp")
        p.op("scalar", lambda e: e.activation(dtpcol[:, 0:1], ssm_f[:, 64:65], AF.Exp),
             inc="Ssp")
        p.op("scalar", lambda e: e.activation(dtprow, dtprow, AF.Ln, bias=1.0),
             inc="Ssp")
        p.op("scalar", lambda e: e.activation(dtpcol[:, 0:1], dtpcol[:, 0:1], AF.Ln,
                                              bias=1.0), inc="Ssp")
        p.wait("vector", "Ssp")
        p.op("vector", lambda e: e.tensor_tensor_scan(crow, onesr, dtprow, 0.0,
                                                      OP.mult, OP.add), inc="Vsp")
        p.wait("vector", *hist["DxcT"][gc])
        p.wait("vector", "Tdh")
        p.op("vector", lambda e, b=gc % 4, par=par: e.tensor_scalar(
            x2_2[par], xcT4[b], dtpcol[:, 0:1], None, OP.mult), inc="Vx2")
        hist["Vx2"].append(p.c("Vx2"))

    def prep_cT(gc):
        p.wait("tensor", "Vsp")
        p.wait("tensor", "Ses")
        p.wait("tensor", "Vpe")
        p.op("tensor", lambda e: e.matmul(v_pe1, arow16, crow, start=True, stop=True),
             inc="Tpe")
        p.op("tensor", lambda e: e.matmul(v_pe2, crow, neg1r[:, 0:1],
                                          start=True, stop=False), inc="Tpe")
        p.op("tensor", lambda e: e.matmul(v_pe2, onesr, crow[:, 127:128],
                                          start=False, stop=True), inc="Tpe")
        p.wait("tensor", "Vdl")
        p.op("tensor", lambda e: e.matmul(v_dl, onesr, crow, start=True, stop=False),
             inc="Tdl")
        p.op("tensor", lambda e: e.matmul(v_dl, crow, neg1r, start=False, stop=True),
             inc="Tdl")

    def prep_cVS(gc):
        par = gc % 2
        p.wait("vector", "Tpe")
        p.wait("vector", "Tdl")
        p.op("vector", lambda e: e.tensor_copy(stat[:, 8:9], v_pe2), inc="Vpe")
        p.wait("scalar", "Tpe")
        p.wait("scalar", "Vpe")
        p.op("scalar", lambda e: e.activation(es_f, v_pe1, AF.Exp), inc="Ses")
        p.op("scalar", lambda e: e.activation(efac, A_bc, AF.Exp,
                                              scale=stat[:, 8:9]), inc="Ses")
        p.wait("vector", "Ses")
        if gc >= 2:
            p.wait("vector", "Ty", hist["Ty"][gc - 2])
        p.wait("vector", "Tdh")
        p.op("vector", lambda e, par=par: e.tensor_tensor(
            ceT2[par], ssmT[32:48, :], es_f, OP.mult))
        p.op("vector", lambda e, par=par: e.tensor_tensor(
            bex2[par], ssm_f[:, 0:16], efac, OP.mult))
        p.op("vector", lambda e, par=par: e.tensor_copy(
            dcol2[32:48, par:par + 1], es_f[:, 127:128]), inc="Vce")
        hist["Vce"].append(p.c("Vce"))
        p.wait("vector", "Tdl")
        p.op("vector", lambda e: e.tensor_tensor(dl_f, v_dl, maskut, OP.add),
             inc="Vdl")

    def prep_d_cb(gc, h):
        if h == 0:
            p.wait("tensor", "Vpp")
            p.wait("tensor", "Dbc")
        else:
            p.wait("tensor", "Vpp", 2 * gc + 1)
        for si in range(8):
            s_ = h * 8 + si
            p.op("tensor", lambda e, si=si, s_=s_: e.matmul(
                P2[:, si * 128:(si + 1) * 128],
                bcrow[:, s_ * 128:(s_ + 1) * 128],
                bcrow[:, (32 + s_) * 128:(33 + s_) * 128],
                start=True, stop=True),
                 inc=("Tcb" if si == 7 else None))

    def prep_dV(gc):
        gidx, cc = divmod(gc, NCH)
        par = gc % 2
        t0 = cc * T
        dl_rep = dl_f.rearrange("p t -> p () t").broadcast_to([128, 8, 128])
        for h in range(2):
            p.wait("vector", "Sea")
            a_rep = A_bc[:, h * 8:(h + 1) * 8].rearrange(
                "p s -> p s ()").broadcast_to([128, 8, 128])
            p.op("vector", lambda e, a_rep=a_rep: e.tensor_tensor(
                dall.rearrange("p (s t) -> p s t", t=128), dl_rep, a_rep, OP.mult),
                 inc="Vda")
            p.wait("scalar", "Vda")
            p.op("scalar", lambda e: e.activation(eall, dall, AF.Exp), inc="Sea")
            p.wait("vector", "Tcb", 2 * gc + h + 1)
            p.wait("vector", "Sea")
            p.op("vector", lambda e: e.tensor_tensor(
                p_sb, eall, P2[:, 0:1024], OP.mult), inc="Vpp")
            p.op("vector", lambda e: e.tensor_tensor(
                p_sb[:, 0:512], p_sb[:, 0:512], p_sb[:, 512:1024], OP.add))
            p.op("vector", lambda e: e.tensor_tensor(
                p_sb[:, 0:256], p_sb[:, 0:256], p_sb[:, 256:512], OP.add))
            if h == 0:
                if gc >= 2:
                    p.wait("vector", "Ty", hist["Ty"][gc - 2])
                p.op("vector", lambda e, par=par: e.tensor_tensor(
                    m_2[par], p_sb[:, 0:128], p_sb[:, 128:256], OP.add))
            else:
                p.op("vector", lambda e: e.tensor_tensor(
                    p_sb[:, 0:128], p_sb[:, 0:128], p_sb[:, 128:256], OP.add))
                p.op("vector", lambda e, par=par: e.tensor_tensor(
                    m_2[par], m_2[par], p_sb[:, 0:128], OP.add), inc="Vm")
        hist["Vm"].append(p.c("Vm"))
        p.wait("vector", "Scv")
        for m in range(16):
            p.op("vector", lambda e, m=m, t0=t0, par=par: e.tensor_scalar(
                dterm2[par].rearrange("p (a t) -> p a t", t=128)[:, m, :],
                xcs_b[:, m, t0:t0 + T], Dm[:, m:m + 1], None, OP.mult),
                 inc=("VdtX" if m == 15 else None))
        hist["Vdt"].append(p.c("VdtX"))

    def heavy_memset(hc):
        p.wait("vector", "Tdh")
        p.wait("vector", "Ty")
        p.op("vector", lambda e: e.memset(H_sb, 0.0), inc="Vhu")

    def heavy_T1(hc):
        par = hc % 2
        p.wait("tensor", "Vm", hist["Vm"][hc])
        p.wait("tensor", "Vx2", hist["Vx2"][hc])
        p.wait("tensor", "Vce", hist["Vce"][hc])
        p.wait("tensor", "Vhu")
        p.wait("tensor", "Vhs")
        if hc >= 1:
            p.wait("tensor", "Vr", hist["Vr"][hc - 1])
        for h in range(2):
            if h == 0:
                p.wait("tensor", "Vy")
            else:
                p.wait("tensor", "Vy", 2 * hc + 1)
            for m in range(8):
                md = h * 8 + m
                p.op("tensor", lambda e, m=m, md=md, par=par: e.matmul(
                    P3[:, m * 128:(m + 1) * 128],
                    x2_2[par][:, md * 128:(md + 1) * 128],
                    m_2[par], start=True, stop=False))
                p.op("tensor", lambda e, m=m, md=md, par=par: e.matmul(
                    P3[:, m * 128:(m + 1) * 128], H_sb[:, md * 128:(md + 1) * 128],
                    ceT2[par], start=False, stop=True),
                     inc=("Ty" if m == 7 else None))
        hist["Ty"].append(p.c("Ty"))

    def heavy_T2(hc, h):
        par = hc % 2
        if h == 0:
            p.wait("tensor", "Vy", 2 * (hc + 1))
        else:
            p.wait("tensor", "Vhs", 2 * hc + 1)
        for q_ in range(2):
            p.op("tensor", lambda e, h=h, q_=q_, par=par: e.matmul(
                P3[32:48, q_ * 512:(q_ + 1) * 512], bex2[par],
                x2_2[par][:, h * 1024 + q_ * 512:h * 1024 + (q_ + 1) * 512],
                start=True, stop=True), inc=("Tdh" if q_ == 1 else None))

    def heavy_T5(hc):
        p.wait("tensor", "Vyf", hc + 1)
        p.wait("tensor", "Vhs", 2 * (hc + 1))
        for m in range(16):
            for n in range(2):
                p.op("tensor", lambda e, m=m, n=n: e.matmul(
                    P3[:, n * 512:(n + 1) * 512], yT_sb[:, m * 128:(m + 1) * 128],
                    woutT[:, m, n * 512:(n + 1) * 512],
                    start=(m == 0), stop=(m == 15)),
                     inc=("Tmo" if (m == 15 and n == 1) else None))

    def heavy_V(hc):
        gidx, cc = divmod(hc, NCH)
        par = hc % 2
        t0 = cc * T
        p.wait("vector", "Sst")
        p.wait("vector", "Ty", 2 * hc + 1)
        p.op("vector", lambda e, par=par: e.tensor_tensor(
            yT_sb[:, 0:1024], P3[:, 0:1024], dterm2[par][:, 0:1024], OP.add),
             inc="Vy")
        p.wait("vector", "Ty", hist["Ty"][hc])
        p.op("vector", lambda e, par=par: e.tensor_tensor(
            yT_sb[:, 1024:2048], P3[:, 0:1024], dterm2[par][:, 1024:2048], OP.add),
             inc="Vy")
        hist["Vy"].append(p.c("Vy"))
        p.op("vector", lambda e, par=par: e.tensor_scalar(
            H_sb, H_sb, dcol2[32:48, par:par + 1], None, OP.mult), inc="Vhu")
        p.wait("vector", "Tdh", 2 * hc + 1)
        for q_ in range(2):
            p.op("vector", lambda e, q_=q_: e.tensor_tensor(
                H_sb[:, q_ * 512:(q_ + 1) * 512],
                H_sb[:, q_ * 512:(q_ + 1) * 512],
                P3[32:48, q_ * 512:(q_ + 1) * 512], OP.add),
                 inc=("Vhs" if q_ == 1 else None))
        yT3 = yT_sb.rearrange("p (a t) -> p a t", t=T)
        p.wait("vector", "Szv", 16 * (gidx + 1))
        p.op("vector", lambda e, yT3=yT3, t0=t0: e.tensor_tensor(
            yT3, yT3, z_b[:, :, t0:t0 + T], OP.mult), inc="Vyf")
        p.wait("vector", "Tdh", 2 * (hc + 1))
        for q_ in range(2):
            p.op("vector", lambda e, q_=q_: e.tensor_tensor(
                H_sb[:, 1024 + q_ * 512:1024 + (q_ + 1) * 512],
                H_sb[:, 1024 + q_ * 512:1024 + (q_ + 1) * 512],
                P3[32:48, q_ * 512:(q_ + 1) * 512], OP.add),
                 inc=("Vhs" if q_ == 1 else None))

    def heavy_Q(hc):
        gidx, cc = divmod(hc, NCH)
        r, j = divmod(gidx, NG)
        par = hc % 2
        gtok = j * G + cc * T
        p.wait("vector", "Tmo")
        p.wait("vector", *hist["Dxn"][hc])
        p.op("vector", lambda e, b=par: e.tensor_tensor(
            x_c2[b], P3[:, 0:1024], x_c2[b], OP.add), inc="Vr")
        hist["Vr"].append(p.c("Vr"))
        p.wait("scalar", "Vr")
        p.wait("scalar", "Dx1")
        rr = x_c2[par]
        dmp = yT_sb.bitcast(F32)[:, 0:1024]
        p.op("scalar", lambda e, rr=rr, dmp=dmp: e.activation(
            dmp, rr, AF.Identity, accum_out=stat[:, 0:1]), inc="Sst")
        p.op("scalar", lambda e, rr=rr, dmp=dmp: e.activation(
            dmp, rr, AF.Square, accum_out=stat[:, 1:2]), inc="Sst")
        p.wait("vector", "Sst")
        ln_tail(stat, rr, rr)
        p.wait("sync", "Sx1")
        dma("sync", s_x1[r, gtok:gtok + T, :], rr, "Dx1")
        hist["Dx1"].append(p.c("Dx1"))

    p.wait("tensor", "Dw")
    hist["nxT"][1] = dma(
        "gpsimd", xT2[1][:],
        d_xT[0, :, G:2 * G].rearrange("(a p) t -> p a t", p=128), "DxT1")
    for i in range(4):
        wring_load(i)
    p.op("vector", lambda e: e.memset(xi_b[:, :, 0:4], 0.0))
    for i in range(32):
        emit_sec_mm(i)
        if i >= 1:
            emit_sec_evac(i - 1)
    emit_sec_evac(31)
    conv_tail(0)

    for gc in range(64):
        gidx, cc = divmod(gc, NCH)
        hc = gc - 2
        r, j = divmod(gidx, NG)
        if hc >= 0 and hc % 16 == 0:
            heavy_memset(hc)
        if hc >= 0:
            heavy_T1(hc)
            heavy_T2(hc, 0)
            heavy_T2(hc, 1)
        if hc >= 0:
            heavy_V(hc)
        drain_evacs()
        if cc == 0 and gidx + 1 < 16:
            p.wait("vector", "Vcv")
            p.wait("vector", "Vxz")
            if (gidx + 1) % NG == 0:
                p.op("vector", lambda e: e.memset(xi_b[:, :, 0:4], 0.0))
            else:
                p.op("vector", lambda e: e.tensor_copy(xi_b[:, :, 0:4],
                                                       xi_b[:, :, 512:516]))
        prep_aT(gc)
        prep_b(gc)
        xi_pair(gidx, cc, 0)
        prep_cT(gc)
        prep_d_cb(gc, 0)
        prep_d_cb(gc, 1)
        if hc >= 0:
            heavy_T5(hc)
        prep_cVS(gc)
        drain_evacs()
        prep_dV(gc)
        if hc >= 0:
            heavy_Q(hc)
        par = gc % 2
        if gc >= 2:
            p.wait("sync", "Vr", hist["Vr"][gc - 2])
            p.wait("sync", "Dx1", hist["Dx1"][gc - 2])
        n = dma("sync", x_c2[par][:], d_x[r, j * G + cc * T:j * G + cc * T + T, :],
                f"Dxn{par}")
        hist["Dxn"].append((f"Dxn{par}", n))
        if cc == 1 and gidx >= 1:
            z_batch(gidx)
        if cc == 2 and gidx + 2 < 16:
            nr, nj = divmod(gidx + 2, NG)
            p.wait("gpsimd", "Tmx", idx_last[gidx] + 1)
            hist["nxT"][gidx + 2] = dma(
                "gpsimd", xT2[gidx % 2][:],
                d_xT[nr, :, nj * G:(nj + 1) * G]
                .rearrange("(a p) t -> p a t", p=128), f"DxT{gidx % 2}")
        xi_pair(gidx, cc, 1)
        if cc == 3 and gidx + 1 < 16:
            drain_evacs()
            conv_tail(gidx + 1)
        if cc != 3:
            wq_i = NWQPRO + gidx * 3 + cc
            if wq_i < len(wq_tiles):
                wq_tile_quant(wq_i)
                if wq_i + 1 < len(wq_tiles):
                    wq_tile_dma(wq_i + 1)

    for hc in (62, 63):
        heavy_T1(hc)
        heavy_T2(hc, 0)
        heavy_T2(hc, 1)
        heavy_V(hc)
        heavy_T5(hc)
        heavy_Q(hc)

    p.wait("scalar", "Tmx")
    p.wait("scalar", "Tmo")
    p.wait("scalar", "Dwq")
    p.wait("scalar", "Dx1")
    p.wait("scalar", "Vpp")
    p.wait("scalar", "Vyf")
    p.wait("sync", "Dx1")
    p.wait("sync", "Vpp")
    p.wait("tensor", "Vpp")
    p.wait("tensor", "Vr")
    p.wait("tensor", "Vxz")
    p.wait("tensor", "Szv")
    p.wait("vector", "Dx1")

    for k in range(8):
        tdma("scalar", wqgT[:, k, :], s_wqg[:, k * 128:(k + 1) * 128], "Dwtgu")
    for k in range(8):
        tdma("scalar", wquT[:, k, :], s_wqu[:, k * 128:(k + 1) * 128], "Dwtgu")
    for k in range(22):
        tdma("scalar", wqdT[:, k, :], s_wqd[:, k * 128:(k + 1) * 128], "Dwtd")

    xq_bf = q_xn.bitcast(BF16)[:, 0:1024]

    bh = {"Vq": [], "Vr2": [], "Dout": [], "DxqT": {}, "DhqT": {}, "Tgu": {}}

    def b0_quant(gq):
        rq, jq = divmod(gq, NG)
        for cc in range(NCH):
            gtok = jq * G + cc * T
            if bh["Vq"]:
                p.wait("sync", "Vq", bh["Vq"][-1])
            dma("sync", x1q, s_x1[rq, gtok:gtok + T, :], "Dx1q")
            need = p.c("Dx1q")
            p.wait("vector", "Dx1q", need)
            p.wait("vector", "Dxq")
            sd = emit_quant(x1q, xq_bf, DM)
            bh["Vq"].append(sd)
            p.wait("sync", "Vq")
            dma("sync", s_xq[rq, gtok:gtok + T, :], xq_bf, "Dxq")

    def b1_xqT(gq, half):
        rq, jq = divmod(gq, NG)
        gh = 2 * gq + half
        p.wait("scalar", "Dxq")
        if (gh - 2) >= 0 and (gh - 2) in bh["Tgu"]:
            p.wait("scalar", "Tgu", bh["Tgu"][gh - 2])
        t0 = jq * G + half * 256
        for k in range(8):
            n = tdma("scalar", xqTh[gh % 2][:, k, :],
                     s_xq[rq, t0:t0 + 256, k * 128:(k + 1) * 128], f"DxqT{gh % 2}")
        bh["DxqT"][gh] = (f"DxqT{gh % 2}", n)

    def b2_gateup(gq, cc):
        gh = 2 * gq + cc // 2
        ct = (cc % 2) * T
        p.wait("tensor", *bh["DxqT"][gh])
        p.wait("tensor", "Dwtgu", 16 * 16)
        p.wait("vector", "Dxq")
        subs = [(0, 1024), (1024, 1024), (2048, 768)]
        for s0, sw in subs:
            p.wait("tensor", "Vgu")
            p.wait("tensor", "Sgu")
            nsub = (sw + 511) // 512
            for k in range(8):
                for n_ in range(nsub):
                    nw = min(512, sw - n_ * 512)
                    p.op("tensor", lambda e, k=k, n_=n_, nw=nw, s0=s0, b=gh % 2, ct=ct:
                         e.matmul(P1[:, n_ * 512:n_ * 512 + nw],
                                  xqTh[b][:, k, ct:ct + T],
                                  wqgT[:, k, s0 + n_ * 512:s0 + n_ * 512 + nw],
                                  start=(k == 0), stop=(k == 7)))
                for n_ in range(nsub):
                    nw = min(512, sw - n_ * 512)
                    p.op("tensor", lambda e, k=k, n_=n_, nw=nw, s0=s0, b=gh % 2, ct=ct:
                         e.matmul(P2[:, n_ * 512:n_ * 512 + nw],
                                  xqTh[b][:, k, ct:ct + T],
                                  wquT[:, k, s0 + n_ * 512:s0 + n_ * 512 + nw],
                                  start=(k == 0), stop=(k == 7)),
                         inc=("Tgu" if (k == 7 and n_ == nsub - 1) else None))
            p.wait("scalar", "Tgu")
            p.wait("scalar", "Vgu")
            p.op("scalar", lambda e, sw=sw: e.activation(sg_sb[:, 0:sw], P1[:, 0:sw],
                                                         AF.Silu), inc="Sgu")
            p.wait("vector", "Sgu")
            p.wait("vector", "Dhq")
            p.op("vector", lambda e, sw=sw: e.tensor_tensor(
                sg_sb[:, 0:sw], sg_sb[:, 0:sw], P2[:, 0:sw], OP.mult), inc="Vgu")
            emit_quant(sg_sb[:, 0:sw], hq_sub[:, 0:sw], sw)
            p.wait("sync", "Vq")
            if (gq - 2) in bh["DhqT"]:
                p.wait("sync", "DhqT", bh["DhqT"][gq - 2])
            dma("sync", s_hq[gq % 2, cc * T:(cc + 1) * T, s0:s0 + sw],
                hq_sub[:, 0:sw], "Dhq")
        bh["Tgu"][gh] = p.c("Tgu")

    def b3_hqT(gq):
        p.wait("scalar", "Dhq")
        p.wait("scalar", "Tdn")
        for k in range(22):
            tdma("scalar", hqT_b[:, k, :],
                 s_hq[gq % 2, :, k * 128:(k + 1) * 128], "DhqT")
        bh["DhqT"][gq] = p.c("DhqT")

    def b4_down(gq, cc):
        rq, jq = divmod(gq, NG)
        gtok = jq * G + cc * T
        p.wait("sync", "Sx1")
        dma("sync", x1l, s_x1[rq, gtok:gtok + T, :], "Dx1b")
        need = p.c("Dx1b")
        p.wait("tensor", "DhqT", bh["DhqT"][gq])
        p.wait("tensor", "Dwtd", 22 * 16)
        p.wait("tensor", "Vr2")
        for k in range(22):
            for n_ in range(2):
                p.op("tensor", lambda e, k=k, n_=n_, cc=cc: e.matmul(
                    P3[:, n_ * 512:(n_ + 1) * 512], hqT_b[:, k, cc * T:(cc + 1) * T],
                    wqdT[:, k, n_ * 512:(n_ + 1) * 512],
                    start=(k == 0), stop=(k == 21)),
                     inc=("Tdn" if (k == 21 and n_ == 1) else None))
        p.wait("vector", "Tdn")
        p.wait("vector", "Dx1b", need)
        p.wait("vector", "Sx1")
        p.op("vector", lambda e: e.tensor_tensor(x1l, P3[:, 0:1024], x1l, OP.add),
             inc="Vr2")
        bh["Vr2"].append(p.c("Vr2"))
        p.wait("scalar", "Vr2")
        if bh["Dout"]:
            p.wait("scalar", "Dout", bh["Dout"][-1])
        p.op("scalar", lambda e: e.activation(out_sb, x1l, AF.Identity,
                                              accum_out=statB[:, 0:1]), inc="Sst")
        p.op("scalar", lambda e: e.activation(out_sb, x1l, AF.Square,
                                              accum_out=statB[:, 1:2]), inc="Sst")
        p.wait("vector", "Sst")
        ln_tail(statB, x1l, out_sb)
        p.wait("sync", "Sx1")
        dma("sync", d_out[rq, gtok:gtok + T, :], out_sb, "Dout")
        bh["Dout"].append(p.c("Dout"))

    NGQ = R * NG
    b0_quant(0)
    b1_xqT(0, 0)
    b1_xqT(0, 1)
    for gq in range(NGQ):
        if gq + 1 < NGQ:
            b0_quant(gq + 1)
        for cc in range(NCH):
            b2_gateup(gq, cc)
            if cc == 1 and gq + 1 < NGQ:
                b1_xqT(gq + 1, 0)
            if cc == 3 and gq + 1 < NGQ:
                b1_xqT(gq + 1, 1)
            if gq >= 1:
                b4_down(gq - 1, cc)
        b3_hqT(gq)
    for cc in range(NCH):
        b4_down(NGQ - 1, cc)

    p.wait("sync", "Dout")

    with nc.Block() as block:
        @block.tensor
        def _(eng):
            for fn in p.streams["tensor"]:
                fn(eng)

        @block.vector
        def _(eng):
            for fn in p.streams["vector"]:
                fn(eng)

        @block.scalar
        def _(eng):
            for fn in p.streams["scalar"]:
                fn(eng)

        @block.gpsimd
        def _(eng):
            for fn in p.streams["gpsimd"]:
                fn(eng)

        @block.sync
        def _(eng):
            for fn in p.streams["sync"]:
                fn(eng)

    return nc


def kernel(L0_attn_x, L0_attn_W_in, L0_attn_conv_w, L0_attn_conv_b, L0_attn_W_x,
           L0_attn_A_log, L0_attn_D, L0_attn_W_out, L0_ffn_w_gate, L0_ffn_w_up,
           L0_ffn_w_down):
    x = np.ascontiguousarray(L0_attn_x, dtype=np.float32)
    winT = np.ascontiguousarray(np.asarray(L0_attn_W_in, np.float32).T)
    wxt_raw = np.asarray(L0_attn_W_x, np.float32).T
    wxT = np.zeros((DI, 72), np.float32)
    wxT[:, 0:16] = wxt_raw[:, 0:16]
    wxT[:, 32:48] = wxt_raw[:, 16:32]
    wxT[:, 64] = wxt_raw[:, 32]
    woutT = np.ascontiguousarray(np.asarray(L0_attn_W_out, np.float32).T)
    cw = np.ascontiguousarray(
        np.asarray(L0_attn_conv_w, np.float32)[:, 0, :].reshape(16, 128, 4)
        .transpose(1, 0, 2))
    cb = np.ascontiguousarray(np.asarray(L0_attn_conv_b, np.float32)
                              .reshape(16, 128).T)
    Dmh = np.ascontiguousarray(np.asarray(L0_attn_D, np.float32).reshape(16, 128).T)
    alog = np.ascontiguousarray(L0_attn_A_log, np.float32)
    wg = np.ascontiguousarray(L0_ffn_w_gate, np.float32)
    wu = np.ascontiguousarray(L0_ffn_w_up, np.float32)
    wd = np.ascontiguousarray(L0_ffn_w_down, np.float32)

    nc = build_program()
    in_maps = []
    for i in range(8):
        xs = np.ascontiguousarray(x[4 * i:4 * i + 4])
        xTs = np.ascontiguousarray(xs.transpose(0, 2, 1))
        in_maps.append({
            "x": xs, "xT": xTs, "winT": winT, "wxT": wxT, "woutT": woutT,
            "cw": cw, "cb": cb, "Dm": Dmh, "alog": alog,
            "wg": wg, "wu": wu, "wd": wd,
        })
    res = run_bass_kernel_spmd(nc, in_maps, core_ids=list(range(8)))
    return np.concatenate([r["out"] for r in res.results], axis=0)
